# version 02
# baseline (speedup 1.0000x reference)
import numpy as np
import ml_dtypes
from contextlib import ExitStack

import concourse.bass as bass
import concourse.mybir as mybir
from concourse.bass_utils import run_bass_kernel_spmd
from concourse.alu_op_type import AluOpType

F32 = mybir.dt.float32
BF16 = mybir.dt.bfloat16
AF = mybir.ActivationFunctionType

B, S, D = 4, 4096, 4096
NH, NKV, HD = 32, 8, 128
KC = D // 128
NT = S // 128
OWN = 2048
OC = OWN // 128
NW = 4
GW = S // 512
SCALE = float(HD) ** -0.5
CSTAB = 16.0
EPS = 1e-5
LORA_SCALE = 8.0 / 8.0
OFFS = [[0, 3, 4, 7], [1, 2, 5, 6]]
S1_SERIAL = False
S1_KONLY = False

ENGINES = ("sync", "tensor", "vector", "scalar", "gpsimd")


class Prog:

    def __init__(self):
        self.ops = {e: [] for e in ENGINES}
        self.sem_counts = {}
        self.last_wait = {}

    def emit(self, engine, fn, sem=None, inc=1):
        def op(eng):
            inst = fn(eng)
            if sem is not None and inst is not None:
                inst.then_inc(sem, inc)
        self.ops[engine].append(op)
        if sem is not None:
            self.sem_counts[sem.name] = self.sem_counts.get(sem.name, 0) + inc
            return self.sem_counts[sem.name]
        return None

    def count(self, sem):
        return self.sem_counts.get(sem.name, 0)

    def wait(self, engine, sem, val):
        if val <= 0:
            return
        key = (engine, sem.name)
        if self.last_wait.get(key, -1) >= val:
            return
        self.last_wait[key] = val
        self.ops[engine].append(lambda eng: eng.wait_ge(sem, val))

    def build(self, block):
        for e in ENGINES:
            ops_list = self.ops[e]
            def mk(lst):
                def f(eng):
                    for op in lst:
                        op(eng)
                return f
            getattr(block, e)(mk(ops_list))


class View:

    def __init__(self, t, off, n):
        self.t = t
        self.off = off
        self.n = n

    def __getitem__(self, idx):
        rows, cols = idx
        start = 0 if cols.start is None else cols.start
        stop = self.n if cols.stop is None else cols.stop
        assert 0 <= start <= stop <= self.n
        return self.t[rows, self.off + start:self.off + stop]


class Arena:

    def __init__(self, tensor, cols):
        self.t = tensor
        self.cols = cols
        self.off = 0
        self.marks = []

    def alloc(self, n):
        n_pad = (n + 15) // 16 * 16
        assert self.off + n_pad <= self.cols, f"arena overflow {self.off}+{n_pad}>{self.cols}"
        o = self.off
        self.off += n_pad
        return View(self.t, o, n)

    def push(self):
        self.marks.append(self.off)

    def pop(self):
        self.off = self.marks.pop()


def build_program(debug=False, skip_ln=False, upto=9):
    nc = bass.Bass("TRN2", target_bir_lowering=False)
    SCR = "ExternalOutput"

    xf = nc.dram_tensor("xf", [D, S], F32, kind="ExternalInput").ap()
    xo = nc.dram_tensor("xo", [D, OWN], F32, kind="ExternalInput").ap()
    wq = nc.dram_tensor("wq", [D, NH * HD], F32, kind="ExternalInput").ap()
    wk = nc.dram_tensor("wk", [D, NKV * HD], F32, kind="ExternalInput").ap()
    wv = nc.dram_tensor("wv", [D, NKV * HD], F32, kind="ExternalInput").ap()
    wo = nc.dram_tensor("wo", [NH * HD, D], F32, kind="ExternalInput").ap()
    wupT = nc.dram_tensor("wupT", [D, D], F32, kind="ExternalInput").ap()
    gwt = nc.dram_tensor("gwt", [D, 128], F32, kind="ExternalInput").ap()
    ebias = nc.dram_tensor("ebias", [1, 4], F32, kind="ExternalInput").ap()
    la = nc.dram_tensor("la", [D, 128], F32, kind="ExternalInput").ap()
    lb = nc.dram_tensor("lb", [128, D], F32, kind="ExternalInput").ap()
    masks = nc.dram_tensor("masks", [8, 128, 512], BF16, kind="ExternalInput").ap()
    ident_in = nc.dram_tensor("ident", [128, 128], BF16, kind="ExternalInput").ap()
    rt_in = nc.dram_tensor("rt", [4, 128], BF16, kind="ExternalInput").ap()

    out = nc.dram_tensor("out", [D, OWN], F32, kind="ExternalOutput").ap()

    h_scr = nc.dram_tensor("h_scr", [D, S], BF16,
                           kind=("ExternalInput" if skip_ln else SCR)).ap()
    ho_scr = nc.dram_tensor("ho_scr", [D, OWN], BF16,
                            kind=("ExternalInput" if skip_ln else SCR)).ap()
    q_scr = nc.dram_tensor("q_scr", [NH, 128, OWN], BF16, kind=SCR).ap()
    k_scr = nc.dram_tensor("k_scr", [NKV, 128, S], BF16, kind=SCR).ap()
    v_scr = nc.dram_tensor("v_scr", [NT, 128, NKV * 128], BF16, kind=SCR).ap()
    o_scr = nc.dram_tensor("o_scr", [NW, NH, 128, 512], BF16, kind=SCR).ap()
    xp_scr = nc.dram_tensor("xp_scr", [D, OWN], F32, kind=SCR).ap()
    dbg_hw = nc.dram_tensor("dbg_hw", [2, 128, KC * 256], BF16, kind=SCR).ap() if debug else None
    h2_scr = nc.dram_tensor("h2_scr", [D, OWN], BF16, kind=SCR).ap()
    import os as _os
    if _os.environ.get("Q_DUMMY", "0") == "1":
        q_store_dst = nc.dram_tensor("q_dummy", [NH, 128, OWN], BF16, kind="Internal").ap()
    else:
        q_store_dst = q_scr
    if _os.environ.get("WQ_DUMMY", "0") == "1":
        wq_src = nc.dram_tensor("wq_dummy", [D, NH * HD], F32, kind="Internal").ap()
    else:
        wq_src = wq
    if _os.environ.get("HO_DUMMY", "0") == "1":
        ho_src = nc.dram_tensor("ho_dummy", [D, OWN], BF16, kind="Internal").ap()
    else:
        ho_src = ho_scr

    with ExitStack() as st:
        def sb(name, shape, dt):
            return st.enter_context(nc.sbuf_tensor(name, shape, dt))

        ARENA_BF_COLS = 84480
        ARENA_F32_COLS = 5120
        abf_t = sb("abf", [128, ARENA_BF_COLS], BF16)
        af32_t = sb("af32", [128, ARENA_F32_COLS], F32)

        ones_cf = sb("ones_cf", [128, 1], F32)
        ones_cb = sb("ones_cb", [128, 1], BF16)
        ones_rf = sb("ones_rf", [1, 128], F32)
        ones_rb = sb("ones_rb", [1, 128], BF16)
        nbias = sb("nbias", [128, 1], F32)
        eps_row = sb("eps_row", [1, 1], F32)
        ident_sb = sb("ident_sb", [128, 128], BF16)
        rt_sb = sb("rt_sb", [4, 128], BF16)
        ebias_sb = sb("ebias_sb", [1, 4], BF16)
        masks_sb = sb("masks_sb", [128, 8 * 512], BF16)

        ps = [st.enter_context(nc.psum_tensor(f"ps{i}", [128, 512], F32)) for i in range(7)]
        ps_bt = st.enter_context(nc.psum_tensor("ps_bt", [128, 512], BF16))

        sem = {nm: st.enter_context(nc.semaphore(nm)) for nm in
               ["s_gD", "s_sD", "s_ld", "s_pe", "s_sc", "s_ve", "s_gp", "s_bar"]}
        s_gD, s_sD = sem["s_gD"], sem["s_sD"]
        s_ld = sem["s_ld"]
        s_pe, s_sc, s_ve, s_gp = sem["s_pe"], sem["s_sc"], sem["s_ve"], sem["s_gp"]
        s_bar = sem["s_bar"]

        block = st.enter_context(nc.Block())
        p = Prog()

        abf = Arena(abf_t, ARENA_BF_COLS)
        af32 = Arena(af32_t, ARENA_F32_COLS)

        for fn in [lambda e: e.memset(ones_cf[:, :], 1.0),
                   lambda e: e.memset(ones_rf[:, :], 1.0),
                   lambda e: e.memset(ones_cb[:, :], 1.0),
                   lambda e: e.memset(ones_rb[:, :], 1.0),
                   lambda e: e.memset(nbias[:, :], -CSTAB),
                   lambda e: e.memset(eps_row[:, :], EPS)]:
            p.emit("vector", fn, s_ve, 1)
        SETUP_VE = p.count(s_ve)
        for fn in [lambda e: e.dma_start(out=ident_sb[:, :], in_=ident_in[:, :]),
                   lambda e: e.dma_start(out=rt_sb[:, :], in_=rt_in[:, :])]:
            p.emit("sync", fn, s_ld, 16)
        for rel in range(8):
            p.emit("sync", lambda e, rel=rel: e.dma_start(
                out=masks_sb[:, rel * 512:(rel + 1) * 512], in_=masks[rel, :, :]), s_ld, 16)
        p.emit("gpsimd", lambda e: e.dma_start(out=ebias_sb[:, :], in_=ebias[:, :]), s_gD, 16)
        SETUP_SD = p.count(s_ld)
        SETUP_GD = p.count(s_gD)

        def barrier():
            p.wait("sync", s_sD, p.count(s_sD))
            p.wait("sync", s_ld, p.count(s_ld))
            p.wait("gpsimd", s_gD, p.count(s_gD))
            for e in ENGINES:
                p.emit(e, lambda eng: eng.nop(), s_bar, 1)
            tot = p.count(s_bar)
            for e in ENGINES:
                p.wait(e, s_bar, tot)

        def vchain(fns):
            last = None
            for fn in fns:
                p.wait("vector", s_ve, p.count(s_ve))
                last = p.emit("vector", fn, s_ve, 1)
            return last

        def ln_phase(src, dst, ntok, tag):
            nwin = ntok // 512
            abf.push()
            af32.push()
            xw = [abf.alloc(KC * 512) for _ in range(2)]
            xsq = [abf.alloc(512) for _ in range(4)]
            hb = [abf.alloc(KC * 512) for _ in range(2)]
            m_row = af32.alloc(512)
            msq_row = af32.alloc(512)
            m2_row = af32.alloc(512)
            var_row = af32.alloc(512)
            sd_row = af32.alloc(512)
            rstd_row = af32.alloc(512)
            mr_row = af32.alloc(512)
            rb_sb = af32.alloc(512)
            mrb_sb = af32.alloc(512)
            tnorm = af32.alloc(512)

            ld_done = {}
            st_done = {}
            norm_done = {}
            sq_done = {}
            bc_done = {}
            rows_free = {}
            rowread_done = {}
            sqfold_done = {}

            def emit_load(w):
                if w >= nwin:
                    return
                if w >= 2:
                    p.wait("gpsimd", s_ve, norm_done[w - 2])
                for g4 in range(4):
                    p.emit("gpsimd", lambda e, w=w, g4=g4: e.dma_start(
                        out=xw[w % 2][:, g4 * 8 * 512:(g4 + 1) * 8 * 512],
                        in_=src.rearrange("(kc p) t -> p kc t", p=128)
                            [:, g4 * 8:(g4 + 1) * 8, w * 512:(w + 1) * 512]), s_gD, 16)
                ld_done[w] = p.count(s_gD)

            emit_load(0)
            emit_load(1)

            for w in range(nwin):
                xwc = xw[w % 2]
                hbc = hb[w % 2]
                bank_s = ps[(w % 2) * 2 + 0]
                bank_q = ps[(w % 2) * 2 + 1]
                p.wait("tensor", s_gD, ld_done[w])
                if w >= 2:
                    p.wait("tensor", s_sc, rowread_done[w - 2])
                for k in range(KC):
                    gi = w * KC + k
                    slot = gi % 4
                    p.wait("scalar", s_gD, ld_done[w])
                    if gi >= 4:
                        p.wait("scalar", s_pe, sqfold_done[gi - 4])
                    p.emit("scalar", lambda e, k=k, slot=slot, xwc=xwc: e.square(
                        out=xsq[slot][:, :], in_=xwc[:, k * 512:(k + 1) * 512]), s_sc, 1)
                    sq_done[(w, k)] = p.count(s_sc)
                    p.emit("tensor", lambda e, k=k, xwc=xwc, bank_s=bank_s: e.matmul(
                        bank_s[0:1, :], ones_cb[:, :], xwc[:, k * 512:(k + 1) * 512],
                        start=(k == 0), stop=(k == KC - 1)), s_pe, 1)
                    p.wait("tensor", s_sc, sq_done[(w, k)])
                    cnt = p.emit("tensor", lambda e, k=k, slot=slot, bank_q=bank_q: e.matmul(
                        bank_q[0:1, :], ones_cb[:, :], xsq[slot][:, :],
                        start=(k == 0), stop=(k == KC - 1)), s_pe, 1)
                    sqfold_done[gi] = cnt
                fold_cnt = p.count(s_pe)

                if w >= 1:
                    p.wait("scalar", s_pe, rows_free[w - 1])
                p.wait("scalar", s_pe, fold_cnt)
                p.emit("scalar", lambda e, bank_s=bank_s: e.mul(out=m_row[0:1, :], in_=bank_s[0:1, :], mul=1.0 / D), s_sc, 1)
                p.emit("scalar", lambda e, bank_q=bank_q: e.mul(out=msq_row[0:1, :], in_=bank_q[0:1, :], mul=1.0 / D), s_sc, 1)
                rows_cnt = p.count(s_sc)
                rowread_done[w] = rows_cnt
                p.wait("vector", s_sc, rows_cnt)
                vchain([
                    lambda e: e.tensor_tensor(out=m2_row[0:1, :], in0=m_row[0:1, :], in1=m_row[0:1, :], op=AluOpType.mult),
                    lambda e: e.tensor_tensor(out=var_row[0:1, :], in0=msq_row[0:1, :], in1=m2_row[0:1, :], op=AluOpType.subtract),
                ])
                p.wait("scalar", s_ve, p.count(s_ve))
                p.emit("scalar", lambda e: e.activation(out=sd_row[0:1, :], in_=var_row[0:1, :],
                                                        func=AF.Sqrt, bias=eps_row[:, :]), s_sc, 1)
                p.wait("vector", s_sc, p.count(s_sc))
                vchain([
                    lambda e: e.reciprocal(out=rstd_row[0:1, :], in_=sd_row[0:1, :]),
                    lambda e: e.tensor_tensor(out=mr_row[0:1, :], in0=m_row[0:1, :], in1=rstd_row[0:1, :], op=AluOpType.mult),
                ])
                rows_ve = p.count(s_ve)
                bank_rb = ps[4]
                bank_mb = ps[5]
                p.wait("tensor", s_ve, rows_ve)
                if w >= 1:
                    p.wait("tensor", s_sc, bc_done[w - 1])
                p.emit("tensor", lambda e, bank_rb=bank_rb: e.matmul(
                    bank_rb[:, :], ones_rf[:, :], rstd_row[0:1, :], start=True, stop=True), s_pe, 1)
                p.emit("tensor", lambda e, bank_mb=bank_mb: e.matmul(
                    bank_mb[:, :], ones_rf[:, :], mr_row[0:1, :], start=True, stop=True), s_pe, 1)
                rows_free[w] = p.count(s_pe)
                p.wait("scalar", s_pe, p.count(s_pe))
                if w >= 1:
                    p.wait("scalar", s_ve, norm_done[w - 1])
                p.emit("scalar", lambda e, bank_rb=bank_rb: e.copy(out=rb_sb[:, :], in_=bank_rb[:, :]), s_sc, 1)
                p.emit("scalar", lambda e, bank_mb=bank_mb: e.copy(out=mrb_sb[:, :], in_=bank_mb[:, :]), s_sc, 1)
                bc_done[w] = p.count(s_sc)

                p.wait("vector", s_sc, bc_done[w])
                if w >= 2:
                    p.wait("vector", s_sD, st_done[w - 2])
                for k in range(KC):
                    p.wait("vector", s_ve, p.count(s_ve))
                    p.emit("vector", lambda e, k=k, xwc=xwc: e.tensor_tensor(
                        out=tnorm[:, :], in0=xwc[:, k * 512:(k + 1) * 512], in1=rb_sb[:, :],
                        op=AluOpType.mult), s_ve, 1)
                    p.wait("vector", s_ve, p.count(s_ve))
                    p.emit("vector", lambda e, k=k, hbc=hbc: e.tensor_tensor(
                        out=hbc[:, k * 512:(k + 1) * 512], in0=tnorm[:, :], in1=mrb_sb[:, :],
                        op=AluOpType.subtract), s_ve, 1)
                norm_done[w] = p.count(s_ve)

                p.wait("sync", s_ve, norm_done[w])
                for g4 in range(4):
                    p.emit("sync", lambda e, w=w, g4=g4, hbc=hbc: e.dma_start(
                        out=dst.rearrange("(kc p) t -> p kc t", p=128)
                            [:, g4 * 8:(g4 + 1) * 8, w * 512:(w + 1) * 512],
                        in_=hbc[:, g4 * 8 * 512:(g4 + 1) * 8 * 512]), s_sD, 16)
                st_done[w] = p.count(s_sD)
                emit_load(w + 2)

            p.wait("sync", s_sD, st_done[nwin - 1])
            abf.pop()
            af32.pop()

        if debug and _os.environ.get("KFILL", "0") == "1":
            abf.push()
            kf0 = abf.alloc(S)
            p.wait("vector", s_bar, 0)
            p.emit("vector", lambda e, kf0=kf0: e.memset(kf0[:, :], 777.0), s_ve, 1)
            p.wait("sync", s_ve, p.count(s_ve))
            for kvh in range(NKV):
                p.emit("sync", lambda e, kvh=kvh, kf0=kf0: e.dma_start(
                    out=k_scr[kvh, :, :], in_=kf0[:, :]), s_sD, 16)
            p.wait("sync", s_sD, p.count(s_sD))
            abf.pop()
        barrier()
        if not skip_ln:
            ln_phase(xf, h_scr, S, "ln1")
            barrier()
            ln_phase(xo, ho_scr, OWN, "ln1o")
            barrier()

        def s1_pass(which):
            abf.push()
            hw1 = [abf.alloc(KC * 512) for _ in range(2)]
            ev_sb = [abf.alloc(512) for _ in range(4)]
            w_sb = abf.alloc(KC * 1024)
            wsrc = wk if which == "k" else wv
            for k in range(KC):
                p.emit("gpsimd", lambda e, k=k: e.dma_start(
                    out=w_sb[:, k * 1024:(k + 1) * 1024], in_=wsrc[k * 128:(k + 1) * 128, :]), s_gD, 16)
            W_LOADED = p.count(s_gD)
            h_ld = {}
            used = {}
            ev_free = {}
            ev_done = {}
            ev_rot = 0

            def emit_load(w):
                if w >= 8:
                    return
                if w >= 2:
                    p.wait("sync", s_pe, used[w - 2])
                for g4 in range(2):
                    p.emit("sync", lambda e, w=w, g4=g4: e.dma_start(
                        out=hw1[w % 2][:, g4 * 16 * 512:(g4 + 1) * 16 * 512],
                        in_=h_scr.rearrange("(kc p) t -> p kc t", p=128)
                            [:, g4 * 16:(g4 + 1) * 16, w * 512:(w + 1) * 512]), s_ld, 16)
                h_ld[w] = p.count(s_ld)

            emit_load(0)
            emit_load(1)
            gi = 0
            for w in range(8):
                hwc = hw1[w % 2]
                p.wait("tensor", s_gD, W_LOADED)
                p.wait("tensor", s_ld, h_ld[w])
                if which == "k":
                    groups = [("k", kvh) for kvh in range(NKV)]
                else:
                    groups = [("v", (tt, vh)) for tt in range(4) for vh in range(2)]
                for _, g in groups:
                    bank = ps[gi % 3]
                    if gi >= 3:
                        p.wait("tensor", s_sc, ev_done[gi - 3])
                    if which == "k":
                        kvh = g
                        for k in range(KC):
                            p.emit("tensor", lambda e, k=k, kvh=kvh, bank=bank, hwc=hwc: e.matmul(
                                bank[:, :], w_sb[:, k * 1024 + kvh * 128:k * 1024 + (kvh + 1) * 128],
                                hwc[:, k * 512:(k + 1) * 512], start=(k == 0), stop=(k == KC - 1)), s_pe, 1)
                    else:
                        tt, vh = g
                        for k in range(KC):
                            p.emit("tensor", lambda e, k=k, tt=tt, vh=vh, bank=bank, hwc=hwc: e.matmul(
                                bank[:, :], hwc[:, k * 512 + tt * 128:k * 512 + (tt + 1) * 128],
                                w_sb[:, k * 1024 + vh * 512:k * 1024 + (vh + 1) * 512],
                                start=(k == 0), stop=(k == KC - 1)), s_pe, 1)
                    pe_cnt = p.count(s_pe)
                    slot = ev_rot % 4
                    ev_rot += 1
                    p.wait("scalar", s_pe, pe_cnt)
                    p.wait("scalar", s_sD, ev_free.get(slot, 0))
                    p.emit("scalar", lambda e, bank=bank, slot=slot: e.copy(
                        out=ev_sb[slot][:, :], in_=bank[:, :]), s_sc, 1)
                    ev_done[gi] = p.count(s_sc)
                    p.wait("sync", s_sc, ev_done[gi])
                    if which == "k":
                        p.emit("sync", lambda e, w=w, kvh=g, slot=slot: e.dma_start(
                            out=k_scr[kvh, :, w * 512:(w + 1) * 512], in_=ev_sb[slot][:, :]), s_sD, 16)
                    else:
                        tt, vh = g
                        p.emit("sync", lambda e, w=w, tt=tt, vh=vh, slot=slot: e.dma_start(
                            out=v_scr[w * 4 + tt, :, vh * 512:(vh + 1) * 512], in_=ev_sb[slot][:, :]), s_sD, 16)
                    ev_free[slot] = p.count(s_sD)
                    gi += 1
                used[w] = p.count(s_pe)
                emit_load(w + 2)
            p.wait("sync", s_sD, p.count(s_sD))
            abf.pop()
            barrier()

        s1_pass("k")
        s1_pass("v")

        if upto <= 1:
            p.build(block)
            return nc

        abf.push()
        hw2 = [abf.alloc(KC * 512) for _ in range(2)]
        ev_sb = [abf.alloc(512) for _ in range(4)]
        wq_sb = abf.alloc(KC * 1024)
        wq_ld = {}
        ho_ld = {}
        q_used = {}
        ev_free = {}
        qev_done = {}
        ev_rot = 0
        for qq in range(4):
            if qq >= 1:
                p.wait("gpsimd", s_pe, q_used[(qq - 1, NW - 1)])
            for k in range(KC):
                p.emit("gpsimd", lambda e, k=k, qq=qq: e.dma_start(
                    out=wq_sb[:, k * 1024:(k + 1) * 1024],
                    in_=wq_src[k * 128:(k + 1) * 128, qq * 1024:(qq + 1) * 1024]), s_gD, 16)
                wq_ld[(qq, k)] = p.count(s_gD)
            for w in range(NW):
                if (qq, w) not in ho_ld:
                    if qq * NW + w >= 2:
                        prev = (qq * NW + w - 2)
                        p.wait("sync", s_pe, q_used[(prev // NW, prev % NW)])
                    for g4 in range(2):
                        p.emit("sync", lambda e, w=w, g4=g4: e.dma_start(
                            out=hw2[(qq * NW + w) % 2][:, g4 * 16 * 512:(g4 + 1) * 16 * 512],
                            in_=ho_src.rearrange("(kc p) t -> p kc t", p=128)
                                [:, g4 * 16:(g4 + 1) * 16, w * 512:(w + 1) * 512]), s_ld, 16)
                    ho_ld[(qq, w)] = p.count(s_ld)
                hwc = hw2[(qq * NW + w) % 2]
                p.wait("tensor", s_ld, ho_ld[(qq, w)])
                for mh in ([] if _os.environ.get("S2_NOCOMPUTE", "0") == "1" else range(8)):
                    head = qq * 8 + mh
                    bank = ps[(qq * NW * 8 + w * 8 + mh) % 3]
                    gi = qq * NW * 8 + w * 8 + mh
                    if gi >= 3:
                        p.wait("tensor", s_sc, qev_done.get(gi - 3, 0))
                    for k in range(KC):
                        p.wait("tensor", s_gD, wq_ld[(qq, k)])
                        p.emit("tensor", lambda e, k=k, mh=mh, bank=bank, hwc=hwc: e.matmul(
                            bank[:, :], wq_sb[:, k * 1024 + mh * 128:k * 1024 + (mh + 1) * 128],
                            hwc[:, k * 512:(k + 1) * 512], start=(k == 0), stop=(k == KC - 1)), s_pe, 1)
                    pe_cnt = p.count(s_pe)
                    slot = ev_rot % 4
                    ev_rot += 1
                    p.wait("scalar", s_pe, pe_cnt)
                    p.wait("scalar", s_sD, ev_free.get(slot, 0))
                    p.emit("scalar", lambda e, bank=bank, slot=slot: e.copy(
                        out=ev_sb[slot][:, :], in_=bank[:, :]), s_sc, 1)
                    qev_done[gi] = p.count(s_sc)
                    p.wait("sync", s_sc, qev_done[gi])
                    p.emit("sync", lambda e, head=head, w=w, slot=slot: e.dma_start(
                        out=q_store_dst[head, :, w * 512:(w + 1) * 512], in_=ev_sb[slot][:, :]), s_sD, 16)
                    ev_free[slot] = p.count(s_sD)
                q_used[(qq, w)] = max(p.count(s_pe), 1)
        p.wait("sync", s_sD, p.count(s_sD))
        abf.pop()
        barrier()

        if upto <= 2:
            if debug:
                kcheck = nc.dram_tensor("kcheck", [NKV, 128, S], BF16, kind="ExternalOutput").ap()
                abf.push()
                kb = abf.alloc(S)
                for kvh in range(NKV):
                    prev = p.count(s_ld)
                    p.wait("sync", s_sD, p.count(s_sD))
                    p.emit("sync", lambda e, kvh=kvh, kb=kb: e.dma_start(
                        out=kb[:, :], in_=k_scr[kvh, :, :]), s_ld, 16)
                    p.wait("sync", s_ld, p.count(s_ld))
                    p.emit("sync", lambda e, kvh=kvh, kb=kb: e.dma_start(
                        out=kcheck[kvh, :, :], in_=kb[:, :]), s_sD, 16)
                p.wait("sync", s_sD, p.count(s_sD))
                abf.pop()
            p.build(block)
            return nc

        abf.push()
        af32.push()
        pt = [abf.alloc(512) for _ in range(3)]
        q_sb = [abf.alloc(512) for _ in range(2)]
        o_sb = [abf.alloc(512) for _ in range(2)]
        k_sb = abf.alloc(NKV * S)
        v_sb = abf.alloc(NT * NKV * 128)
        pacc = [af32.alloc(512) for _ in range(2)]
        recip_sb = [af32.alloc(512) for _ in range(2)]
        bc_sb = [af32.alloc(512) for _ in range(2)]
        kv_ld = 0
        for kvh in range(NKV):
            p.emit("sync", lambda e, kvh=kvh: e.dma_start(
                out=k_sb[:, kvh * S:(kvh + 1) * S], in_=k_scr[kvh, :, :]), s_ld, 16)
        for kt in range(NT):
            p.emit("sync", lambda e, kt=kt: e.dma_start(
                out=v_sb[:, kt * 1024:(kt + 1) * 1024], in_=v_scr[kt, :, :]), s_ld, 16)
        kv_ld = p.count(s_ld)

        p.wait("scalar", s_ld, SETUP_SD)
        for _ in range(3):
            p.emit("scalar", lambda e: e.activation(
                out=pt[0][:, :], in_=masks_sb[:, 0:512], func=AF.Exp,
                scale=SCALE, bias=nbias[:, :]), s_sc, 1)

        q_ld = {}
        scores_last = {}
        exp_bank_free = {}
        pt_free = {}
        pacc_free = {}
        oband_free = {}
        fold_bank_free = 0
        bc_bank_free = 0
        osb_free = {}
        qsb_free = {}

        tiles = [(0, 0)] + [(w, h) for w in range(NW) for h in range(NH)]
        def emit_q_load(t):
            w, h = tiles[t]
            p.wait("sync", s_pe, qsb_free.get(t % 2, 0))
            p.emit("sync", lambda e, w=w, h=h, t=t: e.dma_start(
                out=q_sb[t % 2][:, :], in_=q_scr[h, :, w * 512:(w + 1) * 512]), s_ld, 16)
            q_ld[t] = p.count(s_ld)

        emit_q_load(0)
        emit_q_load(1)

        for t, (w, h) in enumerate(tiles):
            kvh = h // 4
            nkt = 8 * (w + 1)
            pool_cnt_tile = 0
            for kt in range(nkt):
                rel = kt - 8 * w
                sbank = ps[kt % 2]
                p.wait("tensor", s_ld, max(kv_ld, q_ld[t]))
                p.wait("tensor", s_sc, exp_bank_free.get(kt % 2, 0))
                cnt = p.emit("tensor", lambda e, kt=kt, kvh=kvh, sbank=sbank, t=t: e.matmul(
                    sbank[:, :], k_sb[:, kvh * S + kt * 128:kvh * S + (kt + 1) * 128],
                    q_sb[t % 2][:, :], start=True, stop=True), s_pe, 1)
                scores_mm = cnt
                slot = kt % 3
                p.wait("scalar", s_pe, scores_mm)
                p.wait("scalar", s_ve, SETUP_VE)
                p.wait("scalar", s_gp, pt_free.get(slot, 0))
                ecnt = p.emit("scalar", lambda e, slot=slot, sbank=sbank: e.activation(
                    out=pt[slot][:, :], in_=sbank[:, :], func=AF.Exp,
                    scale=SCALE, bias=nbias[:, :]), s_sc, 1)
                exp_bank_free[kt % 2] = ecnt
                ready_sem, ready_cnt = s_sc, ecnt
                if rel >= 0:
                    p.wait("vector", s_sc, ecnt)
                    vcnt = p.emit("vector", lambda e, slot=slot, rel=rel: e.tensor_tensor(
                        out=pt[slot][:, :], in0=pt[slot][:, :],
                        in1=masks_sb[:, rel * 512:(rel + 1) * 512], op=AluOpType.mult), s_ve, 1)
                    ready_sem, ready_cnt = s_ve, vcnt
                p.wait("gpsimd", ready_sem, ready_cnt)
                if kt == 0:
                    p.wait("gpsimd", s_pe, pacc_free.get(t % 2, 0))
                    gcnt = p.emit("gpsimd", lambda e, slot=slot, t=t: e.tensor_copy(
                        out=pacc[t % 2][:, :], in_=pt[slot][:, :]), s_gp, 1)
                else:
                    gcnt = p.emit("gpsimd", lambda e, slot=slot, t=t: e.tensor_tensor(
                        out=pacc[t % 2][:, :], in0=pacc[t % 2][:, :], in1=pt[slot][:, :],
                        op=AluOpType.add), s_gp, 1)
                pool_cnt_tile = gcnt
                pt_free[slot] = gcnt
                p.wait("tensor", ready_sem, ready_cnt)
                if kt == 0:
                    p.wait("tensor", s_ve, oband_free.get(t % 2, 0))
                av = p.emit("tensor", lambda e, kt=kt, kvh=kvh, slot=slot, t=t, nkt=nkt: e.matmul(
                    ps[2 + t % 2][:, :], v_sb[:, kt * 1024 + kvh * 128:kt * 1024 + (kvh + 1) * 128],
                    pt[slot][:, :], start=(kt == 0), stop=(kt == nkt - 1)), s_pe, 1)
            scores_last[t] = p.count(s_pe)
            qsb_free[t % 2] = scores_last[t]
            if t + 2 < len(tiles):
                emit_q_load(t + 2)
            p.wait("tensor", s_gp, pool_cnt_tile)
            p.wait("tensor", s_ve, fold_bank_free)
            fcnt = p.emit("tensor", lambda e, t=t: e.matmul(
                ps[4][0:1, :], ones_cf[:, :], pacc[t % 2][:, :], start=True, stop=True), s_pe, 1)
            pacc_free[t % 2] = fcnt
            p.wait("vector", s_pe, fcnt)
            rcnt = p.emit("vector", lambda e, t=t: e.reciprocal(
                out=recip_sb[t % 2][0:1, :], in_=ps[4][0:1, :]), s_ve, 1)
            fold_bank_free = rcnt
            p.wait("tensor", s_ve, rcnt)
            p.wait("tensor", s_sc, bc_bank_free)
            bcnt = p.emit("tensor", lambda e, t=t: e.matmul(
                ps[5][:, :], ones_rf[:, :], recip_sb[t % 2][0:1, :], start=True, stop=True), s_pe, 1)
            p.wait("scalar", s_pe, bcnt)
            ccnt = p.emit("scalar", lambda e, t=t: e.copy(out=bc_sb[t % 2][:, :], in_=ps[5][:, :]), s_sc, 1)
            bc_bank_free = ccnt
            p.wait("vector", s_sc, ccnt)
            p.wait("vector", s_sD, osb_free.get(t % 2, 0))
            ncnt = p.emit("vector", lambda e, t=t: e.tensor_tensor(
                out=o_sb[t % 2][:, :], in0=ps[2 + t % 2][:, :], in1=bc_sb[t % 2][:, :],
                op=AluOpType.mult), s_ve, 1)
            oband_free[t % 2] = ncnt
            if t > 0:
                p.wait("sync", s_ve, ncnt)
                p.emit("sync", lambda e, w=w, h=h, t=t: e.dma_start(
                    out=o_scr[w, h, :, :], in_=o_sb[t % 2][:, :]), s_sD, 16)
                osb_free[t % 2] = p.count(s_sD)
        p.wait("sync", s_sD, p.count(s_sD))
        abf.pop()
        af32.pop()
        barrier()

        if upto <= 3:
            p.build(block)
            return nc

        abf.push()
        af32.push()
        owb = [abf.alloc(NH * 512) for _ in range(2)]
        wo_sb = abf.alloc(KC * 1024)
        xo_sb = [af32.alloc(512) for _ in range(2)]
        xp_sb = [af32.alloc(512) for _ in range(2)]
        wo_ld = {}
        ow_ld = {}
        y_used = {}
        xo_ld = {}
        xo_free = {}
        xp_free = {}
        ev4 = {}
        for qq in range(4):
            if qq >= 1:
                p.wait("gpsimd", s_pe, y_used[(qq - 1, NW - 1)])
            for k in range(KC):
                p.emit("gpsimd", lambda e, k=k, qq=qq: e.dma_start(
                    out=wo_sb[:, k * 1024:(k + 1) * 1024],
                    in_=wo[k * 128:(k + 1) * 128, qq * 1024:(qq + 1) * 1024]), s_gD, 16)
                wo_ld[(qq, k)] = p.count(s_gD)
            for w in range(NW):
                gi = qq * NW + w
                if gi >= 2:
                    prev = gi - 2
                    p.wait("sync", s_pe, y_used[(prev // NW, prev % NW)])
                for g2 in range(2):
                    p.emit("sync", lambda e, w=w, g2=g2, gi=gi: e.dma_start(
                        out=owb[gi % 2][:, g2 * 16 * 512:(g2 + 1) * 16 * 512],
                        in_=o_scr.rearrange("nw nh p c -> nw p nh c")
                            [w, :, g2 * 16:(g2 + 1) * 16, :]), s_ld, 16)
                ow_ld[gi] = p.count(s_ld)
                owc = owb[gi % 2]
                p.wait("tensor", s_ld, ow_ld[gi])
                p.wait("tensor", s_gD, wo_ld[(qq, KC - 1)])
                for mc in range(8):
                    mglob = qq * 8 + mc
                    ti = gi * 8 + mc
                    bank = ps[ti % 3]
                    p.wait("sync", s_ve, xo_free.get(mc % 2, 0))
                    p.emit("sync", lambda e, mglob=mglob, w=w, mc=mc: e.dma_start(
                        out=xo_sb[mc % 2][:, :], in_=xo[mglob * 128:(mglob + 1) * 128,
                                                       w * 512:(w + 1) * 512]), s_ld, 16)
                    xo_ld[ti] = p.count(s_ld)
                    if ti >= 3:
                        p.wait("tensor", s_ve, ev4.get(ti - 3, 0))
                    for h in range(NH):
                        p.emit("tensor", lambda e, h=h, mc=mc, bank=bank, owc=owc: e.matmul(
                            bank[:, :], wo_sb[:, h * 1024 + mc * 128:h * 1024 + (mc + 1) * 128],
                            owc[:, h * 512:(h + 1) * 512], start=(h == 0), stop=(h == NH - 1)), s_pe, 1)
                    pe_cnt = p.count(s_pe)
                    p.wait("vector", s_pe, pe_cnt)
                    p.wait("vector", s_ld, xo_ld[ti])
                    p.wait("vector", s_sD, xp_free.get(mc % 2, 0))
                    vcnt = p.emit("vector", lambda e, mc=mc, bank=bank: e.tensor_tensor(
                        out=xp_sb[mc % 2][:, :], in0=bank[:, :], in1=xo_sb[mc % 2][:, :],
                        op=AluOpType.add), s_ve, 1)
                    ev4[ti] = vcnt
                    xo_free[mc % 2] = vcnt
                    p.wait("sync", s_ve, vcnt)
                    p.emit("sync", lambda e, mglob=mglob, w=w, mc=mc: e.dma_start(
                        out=xp_scr[mglob * 128:(mglob + 1) * 128, w * 512:(w + 1) * 512],
                        in_=xp_sb[mc % 2][:, :]), s_sD, 16)
                    xp_free[mc % 2] = p.count(s_sD)
                y_used[(qq, w)] = p.count(s_pe)
        p.wait("sync", s_sD, p.count(s_sD))
        abf.pop()
        af32.pop()
        barrier()

        if upto <= 4:
            p.build(block)
            return nc

        ln_phase(xp_scr, h2_scr, OWN, "ln2")
        barrier()
        if upto <= 5:
            p.build(block)
            return nc

        abf.push()
        af32.push()
        h2w = [abf.alloc(KC * 512) for _ in range(2)]
        zm_all = abf.alloc(2048)
        gwt_sb = abf.alloc(KC * 128)
        ohT_w = abf.alloc(512)
        oh_sb = abf.alloc(4)
        z_sb = abf.alloc(512)
        zmask_sb = abf.alloc(512)
        wu_sb = abf.alloc(KC * 1024)
        la_sb = abf.alloc(KC * 128)
        lb_sb = abf.alloc(D)
        xo_sb = [af32.alloc(512) for _ in range(2)]
        xp_sb = [af32.alloc(512) for _ in range(2)]
        lg_sb = af32.alloc(4)
        mx1 = af32.alloc(1)
        acc1 = af32.alloc(1)
        tmp1 = af32.alloc(1)
        for k in range(KC):
            p.emit("gpsimd", lambda e, k=k: e.dma_start(
                out=la_sb[:, k * 128:(k + 1) * 128], in_=la[k * 128:(k + 1) * 128, :]), s_gD, 16)
        p.emit("gpsimd", lambda e: e.dma_start(out=lb_sb[:, :], in_=lb[:, :]), s_gD, 16)
        for k in range(KC):
            p.emit("gpsimd", lambda e, k=k: e.dma_start(
                out=gwt_sb[:, k * 128:(k + 1) * 128], in_=gwt[k * 128:(k + 1) * 128, :]), s_gD, 16)
        LORA_LD = p.count(s_gD)

        wu_ld = {}
        h2_ld = {}
        f_used = {}
        zm_done = {}
        xp_ld = {}
        xo_free = {}
        out_free = {}
        ev5 = {}
        for qq in range(4):
            if qq >= 1:
                p.wait("gpsimd", s_pe, f_used[(qq - 1, NW - 1)])
            for k in range(KC):
                p.emit("gpsimd", lambda e, k=k, qq=qq: e.dma_start(
                    out=wu_sb[:, k * 1024:(k + 1) * 1024],
                    in_=wupT[k * 128:(k + 1) * 128, qq * 1024:(qq + 1) * 1024]), s_gD, 16)
                wu_ld[(qq, k)] = p.count(s_gD)
            for w in range(NW):
                gi = qq * NW + w
                if gi >= 2:
                    prev = gi - 2
                    p.wait("sync", s_pe, f_used[(prev // NW, prev % NW)])
                for g4 in range(2):
                    p.emit("sync", lambda e, w=w, g4=g4, gi=gi: e.dma_start(
                        out=h2w[gi % 2][:, g4 * 16 * 512:(g4 + 1) * 16 * 512],
                        in_=h2_scr.rearrange("(kc p) t -> p kc t", p=128)
                            [:, g4 * 16:(g4 + 1) * 16, w * 512:(w + 1) * 512]), s_ld, 16)
                h2_ld[gi] = p.count(s_ld)
                h2c = h2w[gi % 2]
                p.wait("tensor", s_ld, h2_ld[gi])
                p.wait("tensor", s_gD, LORA_LD)

                if qq == 0:
                    p.wait("tensor", s_sc, zm_done.get(("zfree", w - 1), 0))
                    for k in range(KC):
                        p.emit("tensor", lambda e, k=k, h2c=h2c: e.matmul(
                            ps[3][:, :], la_sb[:, k * 128:(k + 1) * 128],
                            h2c[:, k * 512:(k + 1) * 512], start=(k == 0), stop=(k == KC - 1)), s_pe, 1)
                    zpe = p.count(s_pe)
                    p.wait("scalar", s_pe, zpe)
                    p.wait("scalar", s_ve, zm_done.get(("zsb", w - 1), 0))
                    zcp = p.emit("scalar", lambda e: e.mul(out=z_sb[:, :], in_=ps[3][:, :],
                                                           mul=LORA_SCALE), s_sc, 1)
                    zm_done[("zfree", w)] = zcp
                    for tt in range(4):
                        p.wait("tensor", s_sc, zm_done.get(("lgfree", w, tt - 1), 0) if tt > 0
                               else zm_done.get(("lgfree", w - 1, 3), 0))
                        for k in range(KC):
                            p.emit("tensor", lambda e, k=k, tt=tt, h2c=h2c: e.matmul(
                                ps[4][0:128, 0:4], h2c[:, k * 512 + tt * 128:k * 512 + (tt + 1) * 128],
                                gwt_sb[:, k * 128:k * 128 + 4], start=(k == 0), stop=False), s_pe, 1)
                        p.emit("tensor", lambda e: e.matmul(
                            ps[4][0:128, 0:4], ones_rb[:, :], ebias_sb[:, :],
                            start=False, stop=True), s_pe, 1)
                        lgpe = p.count(s_pe)
                        p.wait("scalar", s_pe, lgpe)
                        p.wait("scalar", s_ve, zm_done.get(("lgread", w, tt - 1), 0) if tt > 0
                               else zm_done.get(("lgread", w - 1, 3), 0))
                        lgc = p.emit("scalar", lambda e: e.copy(out=lg_sb[:, :], in_=ps[4][0:128, 0:4]), s_sc, 1)
                        zm_done[("lgfree", w, tt)] = lgc
                        p.wait("vector", s_sc, lgc)
                        vchain([
                            lambda e: e.tensor_tensor(out=mx1[:, :], in0=lg_sb[:, 0:1], in1=lg_sb[:, 1:2], op=AluOpType.max),
                            lambda e: e.tensor_tensor(out=mx1[:, :], in0=mx1[:, :], in1=lg_sb[:, 2:3], op=AluOpType.max),
                            lambda e: e.tensor_tensor(out=mx1[:, :], in0=mx1[:, :], in1=lg_sb[:, 3:4], op=AluOpType.max),
                            lambda e: e.memset(acc1[:, :], 1.0),
                        ])
                        for ei in range(4):
                            vchain([
                                lambda e, ei=ei: e.tensor_tensor(out=tmp1[:, :], in0=lg_sb[:, ei:ei + 1], in1=mx1[:, :], op=AluOpType.is_ge),
                                lambda e, ei=ei: e.tensor_tensor(out=oh_sb[:, ei:ei + 1], in0=tmp1[:, :], in1=acc1[:, :], op=AluOpType.mult),
                            ])
                            if ei < 3:
                                vchain([
                                    lambda e, ei=ei: e.tensor_tensor(out=tmp1[:, :], in0=acc1[:, :], in1=oh_sb[:, ei:ei + 1], op=AluOpType.subtract),
                                    lambda e: e.tensor_copy(out=acc1[:, :], in_=tmp1[:, :]),
                                ])
                        ohve = p.count(s_ve)
                        zm_done[("lgread", w, tt)] = ohve
                        p.wait("tensor", s_ve, ohve)
                        p.wait("tensor", s_ve, zm_done.get(("ohTread", w, tt - 1), 0) if tt > 0
                               else zm_done.get(("ohTread", w - 1, 3), 0))
                        tcnt = p.emit("tensor", lambda e: e.transpose(
                            out=ps_bt[0:4, 0:128], in_=oh_sb[:, :], identity=ident_sb[:, :]), s_pe, 1)
                        p.wait("vector", s_pe, tcnt)
                        ohtc = p.emit("vector", lambda e, tt=tt: e.tensor_copy(
                            out=ohT_w[0:4, tt * 128:(tt + 1) * 128], in_=ps_bt[0:4, 0:128]), s_ve, 1)
                        zm_done[("ohTread", w, tt)] = ohtc
                    p.wait("tensor", s_ve, zm_done[("ohTread", w, 3)])
                    p.wait("tensor", s_sc, zm_done.get(("mfree", w - 1), 0))
                    mcnt = p.emit("tensor", lambda e: e.matmul(
                        ps[5][:, :], rt_sb[:, :], ohT_w[0:4, :], start=True, stop=True), s_pe, 1)
                    p.wait("scalar", s_pe, mcnt)
                    p.wait("scalar", s_ve, zm_done.get(("mread", w - 1), 0))
                    mcp = p.emit("scalar", lambda e: e.copy(out=zmask_sb[:, :], in_=ps[5][:, :]), s_sc, 1)
                    zm_done[("mfree", w)] = mcp
                    p.wait("vector", s_sc, max(mcp, zcp))
                    zmv = p.emit("vector", lambda e, w=w: e.tensor_tensor(
                        out=zm_all[:, w * 512:(w + 1) * 512], in0=z_sb[:, :], in1=zmask_sb[:, :],
                        op=AluOpType.mult), s_ve, 1)
                    zm_done[("zsb", w)] = zmv
                    zm_done[("mread", w)] = zmv
                    zm_done[w] = zmv

                for mc in range(8):
                    mglob = qq * 8 + mc
                    ti = gi * 8 + mc
                    bank = ps[ti % 3]
                    p.wait("sync", s_ve, xo_free.get(mc % 2, 0))
                    p.emit("sync", lambda e, mglob=mglob, w=w, mc=mc: e.dma_start(
                        out=xo_sb[mc % 2][:, :], in_=xp_scr[mglob * 128:(mglob + 1) * 128,
                                                           w * 512:(w + 1) * 512]), s_ld, 16)
                    xp_ld[ti] = p.count(s_ld)
                    if ti >= 3:
                        p.wait("tensor", s_ve, ev5.get(ti - 3, 0))
                    for k in range(KC):
                        p.wait("tensor", s_gD, wu_ld[(qq, k)])
                        p.emit("tensor", lambda e, k=k, mc=mc, bank=bank, h2c=h2c: e.matmul(
                            bank[:, :], wu_sb[:, k * 1024 + mc * 128:k * 1024 + (mc + 1) * 128],
                            h2c[:, k * 512:(k + 1) * 512], start=(k == 0), stop=False), s_pe, 1)
                    p.wait("tensor", s_ve, zm_done[w])
                    p.emit("tensor", lambda e, mglob=mglob, w=w, bank=bank: e.matmul(
                        bank[:, :], lb_sb[:, mglob * 128:(mglob + 1) * 128],
                        zm_all[:, w * 512:(w + 1) * 512], start=False, stop=True), s_pe, 1)
                    pe_cnt = p.count(s_pe)
                    p.wait("vector", s_pe, pe_cnt)
                    p.wait("vector", s_ld, xp_ld[ti])
                    p.wait("vector", s_sD, out_free.get(mc % 2, 0))
                    vcnt = p.emit("vector", lambda e, mc=mc, bank=bank: e.tensor_tensor(
                        out=xp_sb[mc % 2][:, :], in0=bank[:, :], in1=xo_sb[mc % 2][:, :],
                        op=AluOpType.add), s_ve, 1)
                    ev5[ti] = vcnt
                    xo_free[mc % 2] = vcnt
                    p.wait("sync", s_ve, vcnt)
                    p.emit("sync", lambda e, mglob=mglob, w=w, mc=mc: e.dma_start(
                        out=out[mglob * 128:(mglob + 1) * 128, w * 512:(w + 1) * 512],
                        in_=xp_sb[mc % 2][:, :]), s_sD, 16)
                    out_free[mc % 2] = p.count(s_sD)
                f_used[(qq, w)] = p.count(s_pe)
        p.wait("sync", s_sD, p.count(s_sD))
        abf.pop()
        af32.pop()

        p.build(block)

    return nc


def _make_masks(v):
    m = np.zeros((8, 128, 512), dtype=ml_dtypes.bfloat16)
    jj = np.arange(128)[:, None]
    qq = np.arange(128)[None, :]
    for rel in range(8):
        for s_ in range(4):
            off = OFFS[v][s_]
            blk = m[rel, :, s_ * 128:(s_ + 1) * 128]
            if rel < off:
                blk[:, :] = 1
            elif rel == off:
                blk[:, :] = (jj <= qq).astype(ml_dtypes.bfloat16)
    return m


def kernel(L0_attn_x, L0_attn_wq, L0_attn_wk, L0_attn_wv, L0_attn_wo,
           L0_ffn_W_up, L0_ffn_gate_w, L0_ffn_expert_bias, L0_ffn_lora_A,
           L0_ffn_lora_B):
    x = np.asarray(L0_attn_x, dtype=np.float32)
    wq = np.ascontiguousarray(np.asarray(L0_attn_wq, dtype=np.float32))
    wk = np.ascontiguousarray(np.asarray(L0_attn_wk, dtype=np.float32))
    wv = np.ascontiguousarray(np.asarray(L0_attn_wv, dtype=np.float32))
    wo = np.ascontiguousarray(np.asarray(L0_attn_wo, dtype=np.float32))
    wupT = np.ascontiguousarray(np.asarray(L0_ffn_W_up, dtype=np.float32).T)
    gw = np.asarray(L0_ffn_gate_w, dtype=np.float32)
    eb = np.asarray(L0_ffn_expert_bias, dtype=np.float32).reshape(1, 4)
    lA = np.asarray(L0_ffn_lora_A, dtype=np.float32)
    lB = np.asarray(L0_ffn_lora_B, dtype=np.float32)

    gwt = np.zeros((D, 128), dtype=np.float32)
    gwt[:, 0:4] = gw.T
    la = np.zeros((D, 128), dtype=np.float32)
    lb = np.zeros((128, D), dtype=np.float32)
    for e in range(4):
        la[:, 32 * e:32 * e + 8] = lA[e].T
        lb[32 * e:32 * e + 8, :] = lB[e].T
    ident = np.eye(128, dtype=ml_dtypes.bfloat16)
    rt = np.zeros((4, 128), dtype=ml_dtypes.bfloat16)
    for e in range(4):
        rt[e, 32 * e:32 * e + 8] = 1
    masks_v = [_make_masks(0), _make_masks(1)]

    own_tiles = {v: [8 * w + off for w in range(4) for off in OFFS[v]] for v in (0, 1)}

    in_maps = []
    for c in range(8):
        b, v = c // 2, c % 2
        xT = np.ascontiguousarray(x[b].T)
        cols = np.concatenate([np.arange(g * 128, (g + 1) * 128) for g in own_tiles[v]])
        xov = np.ascontiguousarray(xT[:, cols])
        in_maps.append(dict(xf=xT, xo=xov, wq=wq, wk=wk, wv=wv, wo=wo, wupT=wupT,
                            gwt=gwt, ebias=eb, la=la, lb=lb, masks=masks_v[v],
                            ident=ident, rt=rt))

    nc = build_program()
    res = run_bass_kernel_spmd(nc, in_maps, core_ids=list(range(8)))

    result = np.empty((B, S, D), dtype=np.float32)
    for c in range(8):
        b, v = c // 2, c % 2
        sh = res.results[c]["out"]
        for idx, g in enumerate(own_tiles[v]):
            result[b, g * 128:(g + 1) * 128, :] = sh[:, idx * 128:(idx + 1) * 128].T
    return result


# version 07 (final)
# speedup vs baseline: 1.0068x; 1.0068x over previous
import numpy as np
import ml_dtypes
from contextlib import ExitStack

import concourse.bass as bass
import concourse.mybir as mybir
from concourse.bass_utils import run_bass_kernel_spmd
from concourse.alu_op_type import AluOpType

F32 = mybir.dt.float32
BF16 = mybir.dt.bfloat16
AF = mybir.ActivationFunctionType

B, S, D = 4, 4096, 4096
NH, NKV, HD = 32, 8, 128
KC = D // 128
NT = S // 128
OWN = 2048
OC = OWN // 128
NW = 4
GW = S // 512
SCALE = float(HD) ** -0.5
CSTAB = 20.0
EPS = 1e-5
LORA_SCALE = 8.0 / 8.0
OFFS = [[0, 3, 4, 7], [1, 2, 5, 6]]
S1_SERIAL = False
S1_KONLY = False

ENGINES = ("sync", "tensor", "vector", "scalar", "gpsimd")


class Prog:

    def __init__(self):
        self.ops = {e: [] for e in ENGINES}
        self.sem_counts = {}
        self.last_wait = {}

    def emit(self, engine, fn, sem=None, inc=1):
        def op(eng):
            inst = fn(eng)
            if sem is not None and inst is not None:
                inst.then_inc(sem, inc)
        self.ops[engine].append(op)
        if sem is not None:
            self.sem_counts[sem.name] = self.sem_counts.get(sem.name, 0) + inc
            return self.sem_counts[sem.name]
        return None

    def count(self, sem):
        return self.sem_counts.get(sem.name, 0)

    def wait(self, engine, sem, val):
        if val <= 0:
            return
        key = (engine, sem.name)
        if self.last_wait.get(key, -1) >= val:
            return
        self.last_wait[key] = val
        self.ops[engine].append(lambda eng: eng.wait_ge(sem, val))

    def build(self, block):
        for e in ENGINES:
            ops_list = self.ops[e]
            def mk(lst):
                def f(eng):
                    for op in lst:
                        op(eng)
                return f
            getattr(block, e)(mk(ops_list))


class View:

    def __init__(self, t, off, n):
        self.t = t
        self.off = off
        self.n = n

    def __getitem__(self, idx):
        rows, cols = idx
        start = 0 if cols.start is None else cols.start
        stop = self.n if cols.stop is None else cols.stop
        assert 0 <= start <= stop <= self.n
        return self.t[rows, self.off + start:self.off + stop]


class Arena:

    def __init__(self, tensor, cols):
        self.t = tensor
        self.cols = cols
        self.off = 0
        self.marks = []

    def alloc(self, n):
        n_pad = (n + 15) // 16 * 16
        assert self.off + n_pad <= self.cols, f"arena overflow {self.off}+{n_pad}>{self.cols}"
        o = self.off
        self.off += n_pad
        return View(self.t, o, n)

    def push(self):
        self.marks.append(self.off)

    def pop(self):
        self.off = self.marks.pop()


def build_program(debug=False, skip_ln=False, upto=9):
    nc = bass.Bass("TRN2", target_bir_lowering=False)
    SCR = "ExternalOutput"

    xf = nc.dram_tensor("xf", [D, S], F32, kind="ExternalInput").ap()
    xo = nc.dram_tensor("xo", [D, OWN], F32, kind="ExternalInput").ap()
    wq = nc.dram_tensor("wq", [D, NH * HD], F32, kind="ExternalInput").ap()
    wk = nc.dram_tensor("wk", [D, NKV * HD], F32, kind="ExternalInput").ap()
    wv = nc.dram_tensor("wv", [D, NKV * HD], F32, kind="ExternalInput").ap()
    wo = nc.dram_tensor("wo", [NH * HD, D], F32, kind="ExternalInput").ap()
    wupT = nc.dram_tensor("wupT", [D, D], F32, kind="ExternalInput").ap()
    gwt = nc.dram_tensor("gwt", [D, 128], F32, kind="ExternalInput").ap()
    ebias = nc.dram_tensor("ebias", [1, 4], F32, kind="ExternalInput").ap()
    la = nc.dram_tensor("la", [D, 128], F32, kind="ExternalInput").ap()
    lb = nc.dram_tensor("lb", [128, D], F32, kind="ExternalInput").ap()
    masks = nc.dram_tensor("masks", [8, 128, 512], BF16, kind="ExternalInput").ap()
    ident_in = nc.dram_tensor("ident", [128, 128], BF16, kind="ExternalInput").ap()
    rt_in = nc.dram_tensor("rt", [4, 128], BF16, kind="ExternalInput").ap()

    out = nc.dram_tensor("out", [D, OWN], F32, kind="ExternalOutput").ap()

    h_scr = nc.dram_tensor("h_scr", [D, S], BF16,
                           kind=("ExternalInput" if skip_ln else SCR)).ap()
    ho_scr = nc.dram_tensor("ho_scr", [D, OWN], BF16,
                            kind=("ExternalInput" if skip_ln else SCR)).ap()
    q_scr = nc.dram_tensor("q_scr", [NH, 128, OWN], BF16, kind=SCR).ap()
    k_scr = nc.dram_tensor("k_scr", [NKV, 128, S], BF16, kind=SCR).ap()
    v_scr = nc.dram_tensor("v_scr", [NT, 128, NKV * 128], BF16, kind=SCR).ap()
    o_scr = nc.dram_tensor("o_scr", [NW, NH, 128, 512], BF16, kind=SCR).ap()
    xp_scr = nc.dram_tensor("xp_scr", [D, OWN], F32, kind=SCR).ap()
    dbg_hw = nc.dram_tensor("dbg_hw", [2, 128, KC * 256], BF16, kind=SCR).ap() if debug else None
    h2_scr = nc.dram_tensor("h2_scr", [D, OWN], BF16, kind=SCR).ap()
    import os as _os
    if _os.environ.get("Q_DUMMY", "0") == "1":
        q_store_dst = nc.dram_tensor("q_dummy", [NH, 128, OWN], BF16, kind="Internal").ap()
    else:
        q_store_dst = q_scr
    if _os.environ.get("WQ_DUMMY", "0") == "1":
        wq_src = nc.dram_tensor("wq_dummy", [D, NH * HD], F32, kind="Internal").ap()
    else:
        wq_src = wq
    if _os.environ.get("HO_DUMMY", "0") == "1":
        ho_src = nc.dram_tensor("ho_dummy", [D, OWN], BF16, kind="Internal").ap()
    else:
        ho_src = ho_scr

    with ExitStack() as st:
        def sb(name, shape, dt):
            return st.enter_context(nc.sbuf_tensor(name, shape, dt))

        ARENA_BF_COLS = 84480
        ARENA_F32_COLS = 5120
        abf_t = sb("abf", [128, ARENA_BF_COLS], BF16)
        af32_t = sb("af32", [128, ARENA_F32_COLS], F32)

        ones_cf = sb("ones_cf", [128, 1], F32)
        ones_cb = sb("ones_cb", [128, 1], BF16)
        ones_rf = sb("ones_rf", [1, 128], F32)
        ones_rb = sb("ones_rb", [1, 128], BF16)
        nbias = sb("nbias", [128, 1], F32)
        eps_row = sb("eps_row", [1, 1], F32)
        ident_sb = sb("ident_sb", [128, 128], BF16)
        rt_sb = sb("rt_sb", [4, 128], BF16)
        ebias_sb = sb("ebias_sb", [1, 4], BF16)
        masks_sb = sb("masks_sb", [128, 8 * 512], BF16)

        ps = [st.enter_context(nc.psum_tensor(f"ps{i}", [128, 512], F32)) for i in range(7)]
        ps_bt = st.enter_context(nc.psum_tensor("ps_bt", [128, 512], BF16))

        sem = {nm: st.enter_context(nc.semaphore(nm)) for nm in
               ["s_gD", "s_sD", "s_ld", "s_pe", "s_sc", "s_ve", "s_gp", "s_bar"]}
        s_gD, s_sD = sem["s_gD"], sem["s_sD"]
        s_ld = sem["s_ld"]
        s_pe, s_sc, s_ve, s_gp = sem["s_pe"], sem["s_sc"], sem["s_ve"], sem["s_gp"]
        s_bar = sem["s_bar"]

        block = st.enter_context(nc.Block())
        p = Prog()

        abf = Arena(abf_t, ARENA_BF_COLS)
        af32 = Arena(af32_t, ARENA_F32_COLS)

        for fn in [lambda e: e.memset(ones_cf[:, :], 1.0),
                   lambda e: e.memset(ones_rf[:, :], 1.0),
                   lambda e: e.memset(ones_cb[:, :], 1.0),
                   lambda e: e.memset(ones_rb[:, :], 1.0),
                   lambda e: e.memset(nbias[:, :], -CSTAB),
                   lambda e: e.memset(eps_row[:, :], EPS)]:
            p.emit("vector", fn, s_ve, 1)
        SETUP_VE = p.count(s_ve)
        for fn in [lambda e: e.dma_start(out=ident_sb[:, :], in_=ident_in[:, :]),
                   lambda e: e.dma_start(out=rt_sb[:, :], in_=rt_in[:, :])]:
            p.emit("sync", fn, s_ld, 16)
        for rel in range(8):
            p.emit("sync", lambda e, rel=rel: e.dma_start(
                out=masks_sb[:, rel * 512:(rel + 1) * 512], in_=masks[rel, :, :]), s_ld, 16)
        p.emit("gpsimd", lambda e: e.dma_start(out=ebias_sb[:, :], in_=ebias[:, :]), s_gD, 16)
        SETUP_SD = p.count(s_ld)
        SETUP_GD = p.count(s_gD)

        def barrier():
            p.wait("sync", s_sD, p.count(s_sD))
            p.wait("sync", s_ld, p.count(s_ld))
            p.wait("gpsimd", s_gD, p.count(s_gD))
            for e in ENGINES:
                p.emit(e, lambda eng: eng.nop(), s_bar, 1)
            tot = p.count(s_bar)
            for e in ENGINES:
                p.wait(e, s_bar, tot)

        def vchain(fns):
            last = None
            for fn in fns:
                p.wait("vector", s_ve, p.count(s_ve))
                last = p.emit("vector", fn, s_ve, 1)
            return last

        def ln_phase(src, dst, ntok, tag):
            nwin = ntok // 512
            abf.push()
            af32.push()
            xw = [abf.alloc(KC * 512) for _ in range(2)]
            xsq = [abf.alloc(512) for _ in range(4)]
            hb = [abf.alloc(KC * 512) for _ in range(2)]
            m_row = af32.alloc(512)
            msq_row = af32.alloc(512)
            m2_row = af32.alloc(512)
            var_row = af32.alloc(512)
            sd_row = af32.alloc(512)
            rstd_row = af32.alloc(512)
            mr_row = af32.alloc(512)
            rb_sb = af32.alloc(512)
            mrb_sb = af32.alloc(512)
            tnorm = af32.alloc(512)

            ld_done = {}
            st_done = {}
            norm_done = {}
            sq_done = {}
            bc_done = {}
            rows_free = {}
            rowread_done = {}
            sqfold_done = {}

            def emit_load(w):
                if w >= nwin:
                    return
                if w >= 2:
                    p.wait("gpsimd", s_ve, norm_done[w - 2])
                for g4 in range(4):
                    p.emit("gpsimd", lambda e, w=w, g4=g4: e.dma_start(
                        out=xw[w % 2][:, g4 * 8 * 512:(g4 + 1) * 8 * 512],
                        in_=src.rearrange("(kc p) t -> p kc t", p=128)
                            [:, g4 * 8:(g4 + 1) * 8, w * 512:(w + 1) * 512]), s_gD, 16)
                ld_done[w] = p.count(s_gD)

            emit_load(0)
            emit_load(1)

            for w in range(nwin):
                xwc = xw[w % 2]
                hbc = hb[w % 2]
                bank_s = ps[(w % 2) * 2 + 0]
                bank_q = ps[(w % 2) * 2 + 1]
                p.wait("tensor", s_gD, ld_done[w])
                if w >= 2:
                    p.wait("tensor", s_sc, rowread_done[w - 2])
                for k in range(KC):
                    gi = w * KC + k
                    slot = gi % 4
                    p.wait("scalar", s_gD, ld_done[w])
                    if gi >= 4:
                        p.wait("scalar", s_pe, sqfold_done[gi - 4])
                    p.emit("scalar", lambda e, k=k, slot=slot, xwc=xwc: e.square(
                        out=xsq[slot][:, :], in_=xwc[:, k * 512:(k + 1) * 512]), s_sc, 1)
                    sq_done[(w, k)] = p.count(s_sc)
                    p.emit("tensor", lambda e, k=k, xwc=xwc, bank_s=bank_s: e.matmul(
                        bank_s[0:1, :], ones_cb[:, :], xwc[:, k * 512:(k + 1) * 512],
                        start=(k == 0), stop=(k == KC - 1)), s_pe, 1)
                    p.wait("tensor", s_sc, sq_done[(w, k)])
                    cnt = p.emit("tensor", lambda e, k=k, slot=slot, bank_q=bank_q: e.matmul(
                        bank_q[0:1, :], ones_cb[:, :], xsq[slot][:, :],
                        start=(k == 0), stop=(k == KC - 1)), s_pe, 1)
                    sqfold_done[gi] = cnt
                fold_cnt = p.count(s_pe)

                if w >= 1:
                    p.wait("scalar", s_pe, rows_free[w - 1])
                p.wait("scalar", s_pe, fold_cnt)
                p.emit("scalar", lambda e, bank_s=bank_s: e.mul(out=m_row[0:1, :], in_=bank_s[0:1, :], mul=1.0 / D), s_sc, 1)
                p.emit("scalar", lambda e, bank_q=bank_q: e.mul(out=msq_row[0:1, :], in_=bank_q[0:1, :], mul=1.0 / D), s_sc, 1)
                rows_cnt = p.count(s_sc)
                rowread_done[w] = rows_cnt
                p.wait("vector", s_sc, rows_cnt)
                vchain([
                    lambda e: e.tensor_tensor(out=m2_row[0:1, :], in0=m_row[0:1, :], in1=m_row[0:1, :], op=AluOpType.mult),
                    lambda e: e.tensor_tensor(out=var_row[0:1, :], in0=msq_row[0:1, :], in1=m2_row[0:1, :], op=AluOpType.subtract),
                ])
                p.wait("scalar", s_ve, p.count(s_ve))
                p.emit("scalar", lambda e: e.activation(out=sd_row[0:1, :], in_=var_row[0:1, :],
                                                        func=AF.Sqrt, bias=eps_row[:, :]), s_sc, 1)
                p.wait("vector", s_sc, p.count(s_sc))
                vchain([
                    lambda e: e.reciprocal(out=rstd_row[0:1, :], in_=sd_row[0:1, :]),
                    lambda e: e.tensor_tensor(out=mr_row[0:1, :], in0=m_row[0:1, :], in1=rstd_row[0:1, :], op=AluOpType.mult),
                ])
                rows_ve = p.count(s_ve)
                bank_rb = ps[4]
                bank_mb = ps[5]
                p.wait("tensor", s_ve, rows_ve)
                if w >= 1:
                    p.wait("tensor", s_sc, bc_done[w - 1])
                p.emit("tensor", lambda e, bank_rb=bank_rb: e.matmul(
                    bank_rb[:, :], ones_rf[:, :], rstd_row[0:1, :], start=True, stop=True), s_pe, 1)
                p.emit("tensor", lambda e, bank_mb=bank_mb: e.matmul(
                    bank_mb[:, :], ones_rf[:, :], mr_row[0:1, :], start=True, stop=True), s_pe, 1)
                rows_free[w] = p.count(s_pe)
                p.wait("scalar", s_pe, p.count(s_pe))
                if w >= 1:
                    p.wait("scalar", s_ve, norm_done[w - 1])
                p.emit("scalar", lambda e, bank_rb=bank_rb: e.copy(out=rb_sb[:, :], in_=bank_rb[:, :]), s_sc, 1)
                p.emit("scalar", lambda e, bank_mb=bank_mb: e.copy(out=mrb_sb[:, :], in_=bank_mb[:, :]), s_sc, 1)
                bc_done[w] = p.count(s_sc)

                p.wait("vector", s_sc, bc_done[w])
                if w >= 2:
                    p.wait("vector", s_sD, st_done[w - 2])
                for k in range(KC):
                    p.wait("vector", s_ve, p.count(s_ve))
                    p.emit("vector", lambda e, k=k, xwc=xwc: e.tensor_tensor(
                        out=tnorm[:, :], in0=xwc[:, k * 512:(k + 1) * 512], in1=rb_sb[:, :],
                        op=AluOpType.mult), s_ve, 1)
                    p.wait("vector", s_ve, p.count(s_ve))
                    p.emit("vector", lambda e, k=k, hbc=hbc: e.tensor_tensor(
                        out=hbc[:, k * 512:(k + 1) * 512], in0=tnorm[:, :], in1=mrb_sb[:, :],
                        op=AluOpType.subtract), s_ve, 1)
                norm_done[w] = p.count(s_ve)

                p.wait("sync", s_ve, norm_done[w])
                for g4 in range(4):
                    p.emit("sync", lambda e, w=w, g4=g4, hbc=hbc: e.dma_start(
                        out=dst.rearrange("(kc p) t -> p kc t", p=128)
                            [:, g4 * 8:(g4 + 1) * 8, w * 512:(w + 1) * 512],
                        in_=hbc[:, g4 * 8 * 512:(g4 + 1) * 8 * 512]), s_sD, 16)
                st_done[w] = p.count(s_sD)
                emit_load(w + 2)

            p.wait("sync", s_sD, st_done[nwin - 1])
            abf.pop()
            af32.pop()

        if debug and _os.environ.get("KFILL", "0") == "1":
            abf.push()
            kf0 = abf.alloc(S)
            p.wait("vector", s_bar, 0)
            p.emit("vector", lambda e, kf0=kf0: e.memset(kf0[:, :], 777.0), s_ve, 1)
            p.wait("sync", s_ve, p.count(s_ve))
            for kvh in range(NKV):
                p.emit("sync", lambda e, kvh=kvh, kf0=kf0: e.dma_start(
                    out=k_scr[kvh, :, :], in_=kf0[:, :]), s_sD, 16)
            p.wait("sync", s_sD, p.count(s_sD))
            abf.pop()
        barrier()
        if not skip_ln:
            ln_phase(xf, h_scr, S, "ln1")
            barrier()
            ln_phase(xo, ho_scr, OWN, "ln1o")
            barrier()

        def s1_pass(which):
            abf.push()
            hw1 = [abf.alloc(KC * 512) for _ in range(2)]
            ev_sb = [abf.alloc(512) for _ in range(4)]
            w_sb = abf.alloc(KC * 1024)
            wsrc = wk if which == "k" else wv
            for k in range(KC):
                p.emit("gpsimd", lambda e, k=k: e.dma_start(
                    out=w_sb[:, k * 1024:(k + 1) * 1024], in_=wsrc[k * 128:(k + 1) * 128, :]), s_gD, 16)
            W_LOADED = p.count(s_gD)
            h_ld = {}
            used = {}
            ev_free = {}
            ev_done = {}
            ev_rot = 0

            def emit_load(w):
                if w >= 8:
                    return
                if w >= 2:
                    p.wait("sync", s_pe, used[w - 2])
                for g4 in range(2):
                    p.emit("sync", lambda e, w=w, g4=g4: e.dma_start(
                        out=hw1[w % 2][:, g4 * 16 * 512:(g4 + 1) * 16 * 512],
                        in_=h_scr.rearrange("(kc p) t -> p kc t", p=128)
                            [:, g4 * 16:(g4 + 1) * 16, w * 512:(w + 1) * 512]), s_ld, 16)
                h_ld[w] = p.count(s_ld)

            emit_load(0)
            emit_load(1)
            gi = 0
            for w in range(8):
                hwc = hw1[w % 2]
                p.wait("tensor", s_gD, W_LOADED)
                p.wait("tensor", s_ld, h_ld[w])
                if which == "k":
                    groups = [("k", kvh) for kvh in range(NKV)]
                else:
                    groups = [("v", (tt, vh)) for tt in range(4) for vh in range(2)]
                for _, g in groups:
                    bank = ps[gi % 3]
                    if gi >= 3:
                        p.wait("tensor", s_sc, ev_done[gi - 3])
                    if which == "k":
                        kvh = g
                        for k in range(KC):
                            p.emit("tensor", lambda e, k=k, kvh=kvh, bank=bank, hwc=hwc: e.matmul(
                                bank[:, :], w_sb[:, k * 1024 + kvh * 128:k * 1024 + (kvh + 1) * 128],
                                hwc[:, k * 512:(k + 1) * 512], start=(k == 0), stop=(k == KC - 1)), s_pe, 1)
                    else:
                        tt, vh = g
                        for k in range(KC):
                            p.emit("tensor", lambda e, k=k, tt=tt, vh=vh, bank=bank, hwc=hwc: e.matmul(
                                bank[:, :], hwc[:, k * 512 + tt * 128:k * 512 + (tt + 1) * 128],
                                w_sb[:, k * 1024 + vh * 512:k * 1024 + (vh + 1) * 512],
                                start=(k == 0), stop=(k == KC - 1)), s_pe, 1)
                    pe_cnt = p.count(s_pe)
                    slot = ev_rot % 4
                    ev_rot += 1
                    p.wait("scalar", s_pe, pe_cnt)
                    p.wait("scalar", s_sD, ev_free.get(slot, 0))
                    p.emit("scalar", lambda e, bank=bank, slot=slot: e.copy(
                        out=ev_sb[slot][:, :], in_=bank[:, :]), s_sc, 1)
                    ev_done[gi] = p.count(s_sc)
                    p.wait("sync", s_sc, ev_done[gi])
                    if which == "k":
                        p.emit("sync", lambda e, w=w, kvh=g, slot=slot: e.dma_start(
                            out=k_scr[kvh, :, w * 512:(w + 1) * 512], in_=ev_sb[slot][:, :]), s_sD, 16)
                    else:
                        tt, vh = g
                        p.emit("sync", lambda e, w=w, tt=tt, vh=vh, slot=slot: e.dma_start(
                            out=v_scr[w * 4 + tt, :, vh * 512:(vh + 1) * 512], in_=ev_sb[slot][:, :]), s_sD, 16)
                    ev_free[slot] = p.count(s_sD)
                    gi += 1
                used[w] = p.count(s_pe)
                emit_load(w + 2)
            p.wait("sync", s_sD, p.count(s_sD))
            abf.pop()
            barrier()

        s1_pass("k")
        s1_pass("v")

        if upto <= 1:
            p.build(block)
            return nc

        abf.push()
        hw2 = [abf.alloc(KC * 512) for _ in range(2)]
        ev_sb = [abf.alloc(512) for _ in range(4)]
        wq_sb = abf.alloc(KC * 1024)
        wq_ld = {}
        ho_ld = {}
        q_used = {}
        ev_free = {}
        qev_done = {}
        ev_rot = 0
        for qq in range(4):
            if qq >= 1:
                p.wait("gpsimd", s_pe, q_used[(qq - 1, NW - 1)])
            for k in range(KC):
                p.emit("gpsimd", lambda e, k=k, qq=qq: e.dma_start(
                    out=wq_sb[:, k * 1024:(k + 1) * 1024],
                    in_=wq_src[k * 128:(k + 1) * 128, qq * 1024:(qq + 1) * 1024]), s_gD, 16)
                wq_ld[(qq, k)] = p.count(s_gD)
            for w in range(NW):
                if (qq, w) not in ho_ld:
                    if qq * NW + w >= 2:
                        prev = (qq * NW + w - 2)
                        p.wait("sync", s_pe, q_used[(prev // NW, prev % NW)])
                    for g4 in range(2):
                        p.emit("sync", lambda e, w=w, g4=g4: e.dma_start(
                            out=hw2[(qq * NW + w) % 2][:, g4 * 16 * 512:(g4 + 1) * 16 * 512],
                            in_=ho_src.rearrange("(kc p) t -> p kc t", p=128)
                                [:, g4 * 16:(g4 + 1) * 16, w * 512:(w + 1) * 512]), s_ld, 16)
                    ho_ld[(qq, w)] = p.count(s_ld)
                hwc = hw2[(qq * NW + w) % 2]
                p.wait("tensor", s_ld, ho_ld[(qq, w)])
                for mh in ([] if _os.environ.get("S2_NOCOMPUTE", "0") == "1" else range(8)):
                    head = qq * 8 + mh
                    bank = ps[(qq * NW * 8 + w * 8 + mh) % 3]
                    gi = qq * NW * 8 + w * 8 + mh
                    if gi >= 3:
                        p.wait("tensor", s_sc, qev_done.get(gi - 3, 0))
                    for k in range(KC):
                        p.wait("tensor", s_gD, wq_ld[(qq, k)])
                        p.emit("tensor", lambda e, k=k, mh=mh, bank=bank, hwc=hwc: e.matmul(
                            bank[:, :], wq_sb[:, k * 1024 + mh * 128:k * 1024 + (mh + 1) * 128],
                            hwc[:, k * 512:(k + 1) * 512], start=(k == 0), stop=(k == KC - 1)), s_pe, 1)
                    pe_cnt = p.count(s_pe)
                    slot = ev_rot % 4
                    ev_rot += 1
                    p.wait("scalar", s_pe, pe_cnt)
                    p.wait("scalar", s_sD, ev_free.get(slot, 0))
                    p.emit("scalar", lambda e, bank=bank, slot=slot: e.copy(
                        out=ev_sb[slot][:, :], in_=bank[:, :]), s_sc, 1)
                    qev_done[gi] = p.count(s_sc)
                    p.wait("sync", s_sc, qev_done[gi])
                    p.emit("sync", lambda e, head=head, w=w, slot=slot: e.dma_start(
                        out=q_store_dst[head, :, w * 512:(w + 1) * 512], in_=ev_sb[slot][:, :]), s_sD, 16)
                    ev_free[slot] = p.count(s_sD)
                q_used[(qq, w)] = max(p.count(s_pe), 1)
        p.wait("sync", s_sD, p.count(s_sD))
        abf.pop()
        barrier()

        if upto <= 2:
            if debug:
                kcheck = nc.dram_tensor("kcheck", [NKV, 128, S], BF16, kind="ExternalOutput").ap()
                abf.push()
                kb = abf.alloc(S)
                for kvh in range(NKV):
                    prev = p.count(s_ld)
                    p.wait("sync", s_sD, p.count(s_sD))
                    p.emit("sync", lambda e, kvh=kvh, kb=kb: e.dma_start(
                        out=kb[:, :], in_=k_scr[kvh, :, :]), s_ld, 16)
                    p.wait("sync", s_ld, p.count(s_ld))
                    p.emit("sync", lambda e, kvh=kvh, kb=kb: e.dma_start(
                        out=kcheck[kvh, :, :], in_=kb[:, :]), s_sD, 16)
                p.wait("sync", s_sD, p.count(s_sD))
                abf.pop()
            p.build(block)
            return nc

        abf.push()
        af32.push()
        pt = [abf.alloc(512) for _ in range(6)]
        q_sb = [abf.alloc(512) for _ in range(2)]
        o_sb = [abf.alloc(512) for _ in range(2)]
        k_sb = abf.alloc(NKV * S)
        v_sb = abf.alloc(NT * NKV * 128)
        pacc = [af32.alloc(512) for _ in range(2)]
        pnb = af32.alloc(16)
        recip_sb = [af32.alloc(512) for _ in range(2)]
        bc_sb = [af32.alloc(512) for _ in range(2)]
        kv_ld = 0
        for kvh in range(NKV):
            p.emit("sync", lambda e, kvh=kvh: e.dma_start(
                out=k_sb[:, kvh * S:(kvh + 1) * S], in_=k_scr[kvh, :, :]), s_ld, 16)
        for kt in range(NT):
            p.emit("sync", lambda e, kt=kt: e.dma_start(
                out=v_sb[:, kt * 1024:(kt + 1) * 1024], in_=v_scr[kt, :, :]), s_ld, 16)
        kv_ld = p.count(s_ld)

        p.wait("scalar", s_ld, SETUP_SD)
        for _ in range(3):
            p.emit("scalar", lambda e: e.activation(
                out=pt[0][:, :], in_=masks_sb[:, 0:512], func=AF.Exp,
                scale=SCALE, bias=nbias[:, :]), s_sc, 1)

        q_ld = {}
        scores_last = {}
        exp_bank_free = {}
        pt_free = {}
        pacc_free = {}
        oband_free = {}
        fold_bank_free = 0
        bc_bank_free = 0
        osb_free = {}
        qsb_free = {}

        tiles = [(0, 0)] + [(w, h) for w in range(NW) for h in range(NH)]
        def emit_q_load(t):
            w, h = tiles[t]
            p.wait("sync", s_pe, qsb_free.get(t % 2, 0))
            p.emit("sync", lambda e, w=w, h=h, t=t: e.dma_start(
                out=q_sb[t % 2][:, :], in_=q_scr[h, :, w * 512:(w + 1) * 512]), s_ld, 16)
            q_ld[t] = p.count(s_ld)

        emit_q_load(0)
        emit_q_load(1)

        for t, (w, h) in enumerate(tiles):
            kvh = h // 4
            nkt = 8 * (w + 1)
            pool_cnt_tile = 0
            for kt in range(nkt):
                rel = kt - 8 * w
                sbank = ps[[0, 1, 6][kt % 3]]
                p.wait("tensor", s_ld, max(kv_ld, q_ld[t]))
                p.wait("tensor", s_sc, exp_bank_free.get(kt % 3, 0))
                scores_mm = p.emit("tensor", lambda e, kt=kt, kvh=kvh, sbank=sbank, t=t: e.matmul(
                    sbank[:, :], k_sb[:, kvh * S + kt * 128:kvh * S + (kt + 1) * 128],
                    q_sb[t % 2][:, :], start=True, stop=True), s_pe, 1)
                slot = kt % 6
                p.wait("scalar", s_pe, scores_mm)
                p.wait("scalar", s_ve, SETUP_VE)
                p.wait("scalar", s_gp, pt_free.get(slot, 0))
                ecnt = p.emit("scalar", lambda e, slot=slot, sbank=sbank: e.activation(
                    out=pt[slot][:, :], in_=sbank[:, :], func=AF.Exp,
                    scale=SCALE, bias=nbias[:, :]), s_sc, 1)
                exp_bank_free[kt % 3] = ecnt
                ready_sem, ready_cnt = s_sc, ecnt
                if rel >= 0:
                    p.wait("vector", s_sc, ecnt)
                    vcnt = p.emit("vector", lambda e, slot=slot, rel=rel: e.tensor_tensor(
                        out=pt[slot][:, :], in0=pt[slot][:, :],
                        in1=masks_sb[:, rel * 512:(rel + 1) * 512], op=AluOpType.mult), s_ve, 1)
                    ready_sem, ready_cnt = s_ve, vcnt
                p.wait("gpsimd", ready_sem, ready_cnt)
                if kt == 0:
                    p.wait("gpsimd", s_pe, pacc_free.get(t % 2, 0))
                    gcnt = p.emit("gpsimd", lambda e, slot=slot, t=t: e.tensor_copy(
                        out=pacc[t % 2][:, :], in_=pt[slot][:, :]), s_gp, 1)
                else:
                    gcnt = p.emit("gpsimd", lambda e, slot=slot, t=t: e.tensor_tensor(
                        out=pacc[t % 2][:, :], in0=pacc[t % 2][:, :], in1=pt[slot][:, :],
                        op=AluOpType.add), s_gp, 1)
                pool_cnt_tile = gcnt
                pt_free[slot] = gcnt
                if kt == nkt - 1:
                    pool_cnt_tile = p.emit("gpsimd", lambda e, t=t: e.tensor_copy(
                        out=pnb[0:1, :], in_=pacc[t % 2][0:1, 0:16]), s_gp, 1)

                p.wait("tensor", ready_sem, ready_cnt)
                if kt == 0:
                    p.wait("tensor", s_ve, oband_free.get(t % 2, 0))
                av = p.emit("tensor", lambda e, kt=kt, kvh=kvh, slot=slot, t=t, nkt=nkt: e.matmul(
                    ps[2 + t % 2][:, :], v_sb[:, kt * 1024 + kvh * 128:kt * 1024 + (kvh + 1) * 128],
                    pt[slot][:, :], start=(kt == 0), stop=(kt == nkt - 1)), s_pe, 1)
            scores_last[t] = p.count(s_pe)
            qsb_free[t % 2] = scores_last[t]
            if t + 2 < len(tiles):
                emit_q_load(t + 2)
            p.wait("tensor", s_gp, pool_cnt_tile)
            p.wait("tensor", s_ve, fold_bank_free)
            fcnt = p.emit("tensor", lambda e, t=t: e.matmul(
                ps[4][0:1, :], ones_cf[:, :], pacc[t % 2][:, :], start=True, stop=True), s_pe, 1)
            pacc_free[t % 2] = fcnt
            p.wait("vector", s_pe, fcnt)
            rcnt = p.emit("vector", lambda e, t=t: e.reciprocal(
                out=recip_sb[t % 2][0:1, :], in_=ps[4][0:1, :]), s_ve, 1)
            fold_bank_free = rcnt
            p.wait("tensor", s_ve, rcnt)
            p.wait("tensor", s_sc, bc_bank_free)
            bcnt = p.emit("tensor", lambda e, t=t: e.matmul(
                ps[5][:, :], ones_rf[:, :], recip_sb[t % 2][0:1, :], start=True, stop=True), s_pe, 1)
            p.wait("scalar", s_pe, bcnt)
            ccnt = p.emit("scalar", lambda e, t=t: e.copy(out=bc_sb[t % 2][:, :], in_=ps[5][:, :]), s_sc, 1)
            bc_bank_free = ccnt
            p.wait("vector", s_sc, ccnt)
            p.wait("vector", s_sD, osb_free.get(t % 2, 0))
            ncnt = p.emit("vector", lambda e, t=t: e.tensor_tensor(
                out=o_sb[t % 2][:, :], in0=ps[2 + t % 2][:, :], in1=bc_sb[t % 2][:, :],
                op=AluOpType.mult), s_ve, 1)
            oband_free[t % 2] = ncnt
            if t > 0:
                p.wait("sync", s_ve, ncnt)
                p.emit("sync", lambda e, w=w, h=h, t=t: e.dma_start(
                    out=o_scr[w, h, :, :], in_=o_sb[t % 2][:, :]), s_sD, 16)
                osb_free[t % 2] = p.count(s_sD)
        p.wait("sync", s_sD, p.count(s_sD))
        abf.pop()
        af32.pop()
        barrier()

        if upto <= 3:
            p.build(block)
            return nc

        abf.push()
        af32.push()
        owb = [abf.alloc(NH * 512) for _ in range(2)]
        wo_sb = abf.alloc(KC * 1024)
        xo_sb = [af32.alloc(512) for _ in range(2)]
        xp_sb = [af32.alloc(512) for _ in range(2)]
        wo_ld = {}
        ow_ld = {}
        y_used = {}
        xo_ld = {}
        xo_free = {}
        xp_free = {}
        ev4 = {}
        for qq in range(4):
            if qq >= 1:
                p.wait("gpsimd", s_pe, y_used[(qq - 1, NW - 1)])
            for k in range(KC):
                p.emit("gpsimd", lambda e, k=k, qq=qq: e.dma_start(
                    out=wo_sb[:, k * 1024:(k + 1) * 1024],
                    in_=wo[k * 128:(k + 1) * 128, qq * 1024:(qq + 1) * 1024]), s_gD, 16)
                wo_ld[(qq, k)] = p.count(s_gD)
            for w in range(NW):
                gi = qq * NW + w
                if gi >= 2:
                    prev = gi - 2
                    p.wait("sync", s_pe, y_used[(prev // NW, prev % NW)])
                for g2 in range(2):
                    p.emit("sync", lambda e, w=w, g2=g2, gi=gi: e.dma_start(
                        out=owb[gi % 2][:, g2 * 16 * 512:(g2 + 1) * 16 * 512],
                        in_=o_scr.rearrange("nw nh p c -> nw p nh c")
                            [w, :, g2 * 16:(g2 + 1) * 16, :]), s_ld, 16)
                ow_ld[gi] = p.count(s_ld)
                owc = owb[gi % 2]
                p.wait("tensor", s_ld, ow_ld[gi])
                p.wait("tensor", s_gD, wo_ld[(qq, KC - 1)])
                for mc in range(8):
                    mglob = qq * 8 + mc
                    ti = gi * 8 + mc
                    bank = ps[ti % 3]
                    p.wait("sync", s_ve, xo_free.get(mc % 2, 0))
                    p.emit("sync", lambda e, mglob=mglob, w=w, mc=mc: e.dma_start(
                        out=xo_sb[mc % 2][:, :], in_=xo[mglob * 128:(mglob + 1) * 128,
                                                       w * 512:(w + 1) * 512]), s_ld, 16)
                    xo_ld[ti] = p.count(s_ld)
                    if ti >= 3:
                        p.wait("tensor", s_ve, ev4.get(ti - 3, 0))
                    for h in range(NH):
                        p.emit("tensor", lambda e, h=h, mc=mc, bank=bank, owc=owc: e.matmul(
                            bank[:, :], wo_sb[:, h * 1024 + mc * 128:h * 1024 + (mc + 1) * 128],
                            owc[:, h * 512:(h + 1) * 512], start=(h == 0), stop=(h == NH - 1)), s_pe, 1)
                    pe_cnt = p.count(s_pe)
                    p.wait("vector", s_pe, pe_cnt)
                    p.wait("vector", s_ld, xo_ld[ti])
                    p.wait("vector", s_sD, xp_free.get(mc % 2, 0))
                    vcnt = p.emit("vector", lambda e, mc=mc, bank=bank: e.tensor_tensor(
                        out=xp_sb[mc % 2][:, :], in0=bank[:, :], in1=xo_sb[mc % 2][:, :],
                        op=AluOpType.add), s_ve, 1)
                    ev4[ti] = vcnt
                    xo_free[mc % 2] = vcnt
                    p.wait("sync", s_ve, vcnt)
                    p.emit("sync", lambda e, mglob=mglob, w=w, mc=mc: e.dma_start(
                        out=xp_scr[mglob * 128:(mglob + 1) * 128, w * 512:(w + 1) * 512],
                        in_=xp_sb[mc % 2][:, :]), s_sD, 16)
                    xp_free[mc % 2] = p.count(s_sD)
                y_used[(qq, w)] = p.count(s_pe)
        p.wait("sync", s_sD, p.count(s_sD))
        abf.pop()
        af32.pop()
        barrier()

        if upto <= 4:
            p.build(block)
            return nc

        ln_phase(xp_scr, h2_scr, OWN, "ln2")
        barrier()
        if upto <= 5:
            p.build(block)
            return nc

        abf.push()
        af32.push()
        h2w = [abf.alloc(KC * 512) for _ in range(2)]
        zm_all = abf.alloc(2048)
        gwt_sb = abf.alloc(KC * 128)
        ohT_w = abf.alloc(512)
        oh_sb = abf.alloc(4)
        z_sb = abf.alloc(512)
        zmask_sb = abf.alloc(512)
        wu_sb = abf.alloc(KC * 1024)
        la_sb = abf.alloc(KC * 128)
        lb_sb = abf.alloc(D)
        xo_sb = [af32.alloc(512) for _ in range(2)]
        xp_sb = [af32.alloc(512) for _ in range(2)]
        lg_sb = af32.alloc(4)
        mx1 = af32.alloc(1)
        acc1 = af32.alloc(1)
        tmp1 = af32.alloc(1)
        for k in range(KC):
            p.emit("gpsimd", lambda e, k=k: e.dma_start(
                out=la_sb[:, k * 128:(k + 1) * 128], in_=la[k * 128:(k + 1) * 128, :]), s_gD, 16)
        p.emit("gpsimd", lambda e: e.dma_start(out=lb_sb[:, :], in_=lb[:, :]), s_gD, 16)
        for k in range(KC):
            p.emit("gpsimd", lambda e, k=k: e.dma_start(
                out=gwt_sb[:, k * 128:(k + 1) * 128], in_=gwt[k * 128:(k + 1) * 128, :]), s_gD, 16)
        LORA_LD = p.count(s_gD)

        wu_ld = {}
        h2_ld = {}
        f_used = {}
        zm_done = {}
        xp_ld = {}
        xo_free = {}
        out_free = {}
        ev5 = {}
        for qq in range(4):
            if qq >= 1:
                p.wait("gpsimd", s_pe, f_used[(qq - 1, NW - 1)])
            for k in range(KC):
                p.emit("gpsimd", lambda e, k=k, qq=qq: e.dma_start(
                    out=wu_sb[:, k * 1024:(k + 1) * 1024],
                    in_=wupT[k * 128:(k + 1) * 128, qq * 1024:(qq + 1) * 1024]), s_gD, 16)
                wu_ld[(qq, k)] = p.count(s_gD)
            for w in range(NW):
                gi = qq * NW + w
                if gi >= 2:
                    prev = gi - 2
                    p.wait("sync", s_pe, f_used[(prev // NW, prev % NW)])
                for g4 in range(2):
                    p.emit("sync", lambda e, w=w, g4=g4, gi=gi: e.dma_start(
                        out=h2w[gi % 2][:, g4 * 16 * 512:(g4 + 1) * 16 * 512],
                        in_=h2_scr.rearrange("(kc p) t -> p kc t", p=128)
                            [:, g4 * 16:(g4 + 1) * 16, w * 512:(w + 1) * 512]), s_ld, 16)
                h2_ld[gi] = p.count(s_ld)
                h2c = h2w[gi % 2]
                p.wait("tensor", s_ld, h2_ld[gi])
                p.wait("tensor", s_gD, LORA_LD)

                if qq == 0:
                    p.wait("tensor", s_sc, zm_done.get(("zfree", w - 1), 0))
                    for k in range(KC):
                        p.emit("tensor", lambda e, k=k, h2c=h2c: e.matmul(
                            ps[3][:, :], la_sb[:, k * 128:(k + 1) * 128],
                            h2c[:, k * 512:(k + 1) * 512], start=(k == 0), stop=(k == KC - 1)), s_pe, 1)
                    zpe = p.count(s_pe)
                    p.wait("scalar", s_pe, zpe)
                    p.wait("scalar", s_ve, zm_done.get(("zsb", w - 1), 0))
                    zcp = p.emit("scalar", lambda e: e.mul(out=z_sb[:, :], in_=ps[3][:, :],
                                                           mul=LORA_SCALE), s_sc, 1)
                    zm_done[("zfree", w)] = zcp
                    for tt in range(4):
                        p.wait("tensor", s_sc, zm_done.get(("lgfree", w, tt - 1), 0) if tt > 0
                               else zm_done.get(("lgfree", w - 1, 3), 0))
                        for k in range(KC):
                            p.emit("tensor", lambda e, k=k, tt=tt, h2c=h2c: e.matmul(
                                ps[4][0:128, 0:4], h2c[:, k * 512 + tt * 128:k * 512 + (tt + 1) * 128],
                                gwt_sb[:, k * 128:k * 128 + 4], start=(k == 0), stop=False), s_pe, 1)
                        p.emit("tensor", lambda e: e.matmul(
                            ps[4][0:128, 0:4], ones_rb[:, :], ebias_sb[:, :],
                            start=False, stop=True), s_pe, 1)
                        lgpe = p.count(s_pe)
                        p.wait("scalar", s_pe, lgpe)
                        p.wait("scalar", s_ve, zm_done.get(("lgread", w, tt - 1), 0) if tt > 0
                               else zm_done.get(("lgread", w - 1, 3), 0))
                        lgc = p.emit("scalar", lambda e: e.copy(out=lg_sb[:, :], in_=ps[4][0:128, 0:4]), s_sc, 1)
                        zm_done[("lgfree", w, tt)] = lgc
                        p.wait("vector", s_sc, lgc)
                        vchain([
                            lambda e: e.tensor_tensor(out=mx1[:, :], in0=lg_sb[:, 0:1], in1=lg_sb[:, 1:2], op=AluOpType.max),
                            lambda e: e.tensor_tensor(out=mx1[:, :], in0=mx1[:, :], in1=lg_sb[:, 2:3], op=AluOpType.max),
                            lambda e: e.tensor_tensor(out=mx1[:, :], in0=mx1[:, :], in1=lg_sb[:, 3:4], op=AluOpType.max),
                            lambda e: e.memset(acc1[:, :], 1.0),
                        ])
                        for ei in range(4):
                            vchain([
                                lambda e, ei=ei: e.tensor_tensor(out=tmp1[:, :], in0=lg_sb[:, ei:ei + 1], in1=mx1[:, :], op=AluOpType.is_ge),
                                lambda e, ei=ei: e.tensor_tensor(out=oh_sb[:, ei:ei + 1], in0=tmp1[:, :], in1=acc1[:, :], op=AluOpType.mult),
                            ])
                            if ei < 3:
                                vchain([
                                    lambda e, ei=ei: e.tensor_tensor(out=tmp1[:, :], in0=acc1[:, :], in1=oh_sb[:, ei:ei + 1], op=AluOpType.subtract),
                                    lambda e: e.tensor_copy(out=acc1[:, :], in_=tmp1[:, :]),
                                ])
                        ohve = p.count(s_ve)
                        zm_done[("lgread", w, tt)] = ohve
                        p.wait("tensor", s_ve, ohve)
                        p.wait("tensor", s_ve, zm_done.get(("ohTread", w, tt - 1), 0) if tt > 0
                               else zm_done.get(("ohTread", w - 1, 3), 0))
                        tcnt = p.emit("tensor", lambda e: e.transpose(
                            out=ps_bt[0:4, 0:128], in_=oh_sb[:, :], identity=ident_sb[:, :]), s_pe, 1)
                        p.wait("vector", s_pe, tcnt)
                        ohtc = p.emit("vector", lambda e, tt=tt: e.tensor_copy(
                            out=ohT_w[0:4, tt * 128:(tt + 1) * 128], in_=ps_bt[0:4, 0:128]), s_ve, 1)
                        zm_done[("ohTread", w, tt)] = ohtc
                    p.wait("tensor", s_ve, zm_done[("ohTread", w, 3)])
                    p.wait("tensor", s_sc, zm_done.get(("mfree", w - 1), 0))
                    mcnt = p.emit("tensor", lambda e: e.matmul(
                        ps[5][:, :], rt_sb[:, :], ohT_w[0:4, :], start=True, stop=True), s_pe, 1)
                    p.wait("scalar", s_pe, mcnt)
                    p.wait("scalar", s_ve, zm_done.get(("mread", w - 1), 0))
                    mcp = p.emit("scalar", lambda e: e.copy(out=zmask_sb[:, :], in_=ps[5][:, :]), s_sc, 1)
                    zm_done[("mfree", w)] = mcp
                    p.wait("vector", s_sc, max(mcp, zcp))
                    zmv = p.emit("vector", lambda e, w=w: e.tensor_tensor(
                        out=zm_all[:, w * 512:(w + 1) * 512], in0=z_sb[:, :], in1=zmask_sb[:, :],
                        op=AluOpType.mult), s_ve, 1)
                    zm_done[("zsb", w)] = zmv
                    zm_done[("mread", w)] = zmv
                    zm_done[w] = zmv

                for mc in range(8):
                    mglob = qq * 8 + mc
                    ti = gi * 8 + mc
                    bank = ps[ti % 3]
                    p.wait("sync", s_ve, xo_free.get(mc % 2, 0))
                    p.emit("sync", lambda e, mglob=mglob, w=w, mc=mc: e.dma_start(
                        out=xo_sb[mc % 2][:, :], in_=xp_scr[mglob * 128:(mglob + 1) * 128,
                                                           w * 512:(w + 1) * 512]), s_ld, 16)
                    xp_ld[ti] = p.count(s_ld)
                    if ti >= 3:
                        p.wait("tensor", s_ve, ev5.get(ti - 3, 0))
                    for k in range(KC):
                        p.wait("tensor", s_gD, wu_ld[(qq, k)])
                        p.emit("tensor", lambda e, k=k, mc=mc, bank=bank, h2c=h2c: e.matmul(
                            bank[:, :], wu_sb[:, k * 1024 + mc * 128:k * 1024 + (mc + 1) * 128],
                            h2c[:, k * 512:(k + 1) * 512], start=(k == 0), stop=False), s_pe, 1)
                    p.wait("tensor", s_ve, zm_done[w])
                    p.emit("tensor", lambda e, mglob=mglob, w=w, bank=bank: e.matmul(
                        bank[:, :], lb_sb[:, mglob * 128:(mglob + 1) * 128],
                        zm_all[:, w * 512:(w + 1) * 512], start=False, stop=True), s_pe, 1)
                    pe_cnt = p.count(s_pe)
                    p.wait("vector", s_pe, pe_cnt)
                    p.wait("vector", s_ld, xp_ld[ti])
                    p.wait("vector", s_sD, out_free.get(mc % 2, 0))
                    vcnt = p.emit("vector", lambda e, mc=mc, bank=bank: e.tensor_tensor(
                        out=xp_sb[mc % 2][:, :], in0=bank[:, :], in1=xo_sb[mc % 2][:, :],
                        op=AluOpType.add), s_ve, 1)
                    ev5[ti] = vcnt
                    xo_free[mc % 2] = vcnt
                    p.wait("sync", s_ve, vcnt)
                    p.emit("sync", lambda e, mglob=mglob, w=w, mc=mc: e.dma_start(
                        out=out[mglob * 128:(mglob + 1) * 128, w * 512:(w + 1) * 512],
                        in_=xp_sb[mc % 2][:, :]), s_sD, 16)
                    out_free[mc % 2] = p.count(s_sD)
                f_used[(qq, w)] = p.count(s_pe)
        p.wait("sync", s_sD, p.count(s_sD))
        abf.pop()
        af32.pop()

        p.build(block)

    return nc


def _make_masks(v):
    m = np.zeros((8, 128, 512), dtype=ml_dtypes.bfloat16)
    jj = np.arange(128)[:, None]
    qq = np.arange(128)[None, :]
    for rel in range(8):
        for s_ in range(4):
            off = OFFS[v][s_]
            blk = m[rel, :, s_ * 128:(s_ + 1) * 128]
            if rel < off:
                blk[:, :] = 1
            elif rel == off:
                blk[:, :] = (jj <= qq).astype(ml_dtypes.bfloat16)
    return m


def kernel(L0_attn_x, L0_attn_wq, L0_attn_wk, L0_attn_wv, L0_attn_wo,
           L0_ffn_W_up, L0_ffn_gate_w, L0_ffn_expert_bias, L0_ffn_lora_A,
           L0_ffn_lora_B):
    x = np.asarray(L0_attn_x, dtype=np.float32)
    wq = np.ascontiguousarray(np.asarray(L0_attn_wq, dtype=np.float32))
    wk = np.ascontiguousarray(np.asarray(L0_attn_wk, dtype=np.float32))
    wv = np.ascontiguousarray(np.asarray(L0_attn_wv, dtype=np.float32))
    wo = np.ascontiguousarray(np.asarray(L0_attn_wo, dtype=np.float32))
    wupT = np.ascontiguousarray(np.asarray(L0_ffn_W_up, dtype=np.float32).T)
    gw = np.asarray(L0_ffn_gate_w, dtype=np.float32)
    eb = np.asarray(L0_ffn_expert_bias, dtype=np.float32).reshape(1, 4)
    lA = np.asarray(L0_ffn_lora_A, dtype=np.float32)
    lB = np.asarray(L0_ffn_lora_B, dtype=np.float32)

    gwt = np.zeros((D, 128), dtype=np.float32)
    gwt[:, 0:4] = gw.T
    la = np.zeros((D, 128), dtype=np.float32)
    lb = np.zeros((128, D), dtype=np.float32)
    for e in range(4):
        la[:, 32 * e:32 * e + 8] = lA[e].T
        lb[32 * e:32 * e + 8, :] = lB[e].T
    ident = np.eye(128, dtype=ml_dtypes.bfloat16)
    rt = np.zeros((4, 128), dtype=ml_dtypes.bfloat16)
    for e in range(4):
        rt[e, 32 * e:32 * e + 8] = 1
    masks_v = [_make_masks(0), _make_masks(1)]

    own_tiles = {v: [8 * w + off for w in range(4) for off in OFFS[v]] for v in (0, 1)}

    in_maps = []
    for c in range(8):
        b, v = c // 2, c % 2
        xT = np.ascontiguousarray(x[b].T)
        cols = np.concatenate([np.arange(g * 128, (g + 1) * 128) for g in own_tiles[v]])
        xov = np.ascontiguousarray(xT[:, cols])
        in_maps.append(dict(xf=xT, xo=xov, wq=wq, wk=wk, wv=wv, wo=wo, wupT=wupT,
                            gwt=gwt, ebias=eb, la=la, lb=lb, masks=masks_v[v],
                            ident=ident, rt=rt))

    nc = build_program()
    res = run_bass_kernel_spmd(nc, in_maps, core_ids=list(range(8)))

    result = np.empty((B, S, D), dtype=np.float32)
    for c in range(8):
        b, v = c // 2, c % 2
        sh = res.results[c]["out"]
        for idx, g in enumerate(own_tiles[v]):
            result[b, g * 128:(g + 1) * 128, :] = sh[:, idx * 128:(idx + 1) * 128].T
    return result
